# version 28
# baseline (speedup 1.0000x reference)
import numpy as np
from contextlib import ExitStack

import concourse.bass as bass
import concourse.mybir as mybir
from concourse.bass_utils import run_bass_kernel_spmd

F32 = mybir.dt.float32
BF16 = mybir.dt.bfloat16
AF = mybir.ActivationFunctionType
ALU = mybir.AluOpType

H = 1024
I = 4096
T = 8192
NSUB = 256
NSUP = 512
NSTEP = T // NSUP
CH = H // 128
CI = I // 128
EPS = 1e-5
PI = float(np.pi)
TWO_PI = float(2 * np.pi)
RND_M = 12582912.0
DEBUG = False


class Ch:
    def __init__(self, nc, es, name, step=1):
        self.h = es.enter_context(nc.semaphore(name))
        self.n = 0
        self.step = step

    def inc(self, inst):
        inst.then_inc(self.h, self.step)
        self.n += self.step
        return self.n

    def wait(self, eng, mark=None):
        v = self.n if mark is None else mark
        if v and v > 0:
            eng.wait_ge(self.h, v)


def build_program():
    nc = bass.Bass("TRN2", target_bir_lowering=False, debug=False)

    x_d = nc.dram_tensor("x", [T, H], F32, kind="ExternalInput")
    nu_d = nc.dram_tensor("nu_log", [H], F32, kind="ExternalInput")
    th_d = nc.dram_tensor("theta_log", [H], F32, kind="ExternalInput")
    bre_d = nc.dram_tensor("B_re", [H, H], F32, kind="ExternalInput")
    bim_d = nc.dram_tensor("B_im", [H, H], F32, kind="ExternalInput")
    cre_d = nc.dram_tensor("C_re", [H, H], F32, kind="ExternalInput")
    cim_d = nc.dram_tensor("C_im", [H, H], F32, kind="ExternalInput")
    dd_d = nc.dram_tensor("Dv", [H], F32, kind="ExternalInput")
    gl_d = nc.dram_tensor("gamma_log", [H], F32, kind="ExternalInput")
    w1_d = nc.dram_tensor("W1", [H, I], F32, kind="ExternalInput")
    b1_d = nc.dram_tensor("b1", [I], F32, kind="ExternalInput")
    w2_d = nc.dram_tensor("W2", [I, H], F32, kind="ExternalInput")
    b2_d = nc.dram_tensor("b2", [H], F32, kind="ExternalInput")
    out_d = nc.dram_tensor("out", [T, H], F32, kind="ExternalOutput")

    sc_b = [nc.dram_tensor(f"sc_b{c}", [H, H], BF16, kind="Internal")
            for c in range(2)]
    sc_c = [nc.dram_tensor(f"sc_c{c}", [H, H], BF16, kind="Internal")
            for c in range(2)]
    sc_w1 = nc.dram_tensor("sc_w1", [H, I], BF16, kind="Internal")
    sc_w2 = nc.dram_tensor("sc_w2", [I, H], BF16, kind="Internal")
    dbg = {}
    if DEBUG:
        for nm, shp, dt in [("dbg_xn", [128, 2048], BF16),
                            ("dbg_xnf", [128, 4096], BF16),
                            ("dbg_xsre", [128, 4096], BF16),
                            ("dbg_xsim", [128, 4096], BF16),
                            ("dbg_s", [128, 4096], F32),
                            ("dbg_h1", [128, 8192], BF16),
                            ("dbg_g", [128, 2048], F32),
                            ("dbg_cs", [128, 4096], F32),
                            ("dbg_ss", [128, 4096], F32),
                            ("dbg_rt", [128, 8], F32)]:
            dbg[nm] = nc.dram_tensor(nm, shp, dt, kind="ExternalOutput")

    es = ExitStack()

    def sb(name, shape, dtype):
        return es.enter_context(nc.sbuf_tensor(name, shape, dtype))

    def psum(name, shape):
        return es.enter_context(nc.psum_tensor(name, shape, F32))

    wslot = [sb(f"wslot{i}", [128, 8 * 1024], BF16) for i in range(3)]
    cS = sb("cS", [128, CH * NSUP], F32)
    sS = sb("sS", [128, CH * NSUP], F32)
    vv = [sb(f"vv{i}", [128, 2 * NSUP], BF16) for i in range(CH)]
    tmp0 = sb("tmp0", [128, NSUP], F32)
    xsre = sb("xsre", [128, CH * NSUP], BF16)
    xsim = sb("xsim", [128, CH * NSUP], BF16)
    s_b = sb("s_b", [128, CH * NSUP], F32)
    g_b = sb("g_b", [128, CH * NSUB], F32)
    tmp = [tmp0, g_b[:, 0:NSUP]]
    h1 = sb("h1", [128, CI * NSUB], BF16)
    xsl = [sb(f"xsl{i}", [128, 4 * H], F32) for i in range(2)]
    xn_tok = sb("xn_tok", [128, 2 * H], BF16)
    xnf = [sb(f"xnf{i}", [128, CH * NSUP], BF16) for i in range(2)]
    carry = sb("carry", [128, 2 * CH + 16], F32)
    r_tab = sb("r_tab", [128, CH], F32)
    d_tab = sb("d_tab", [128, CH], F32)
    g_tab = sb("g_tab", [128, CH], F32)
    b1_tab = sb("b1_tab", [128, CI], F32)
    b2_tab = sb("b2_tab", [128, CH], F32)
    cD = sb("cD", [128, CH], F32)
    sD = sb("sD", [128, CH], F32)
    ident = sb("ident", [128, 128], F32)
    identb = sb("identb", [128, 128], BF16)

    PA = psum("PA", [128, 2048])
    PB = psum("PB", [128, 2048])
    PP = [PB[:, b * 512:(b + 1) * 512] for b in range(4)]

    TE = Ch(nc, es, "TE")
    VE = Ch(nc, es, "VE")
    SE = Ch(nc, es, "SE")
    GP = Ch(nc, es, "GP")
    SYP = Ch(nc, es, "SYP", step=16)
    BT = [Ch(nc, es, f"BT{i}", step=16) for i in range(4)]
    BST = [Ch(nc, es, f"BST{i}", step=16) for i in range(4)]
    GDP = [Ch(nc, es, f"GDP{i}", step=16) for i in range(2)]
    WST = [Ch(nc, es, f"WST{i}", step=16) for i in range(2)]
    WS = [Ch(nc, es, f"WS{i}", step=16) for i in range(3)]
    XS = [[Ch(nc, es, f"XS{i}_{h}", step=16) for h in range(2)]
          for i in range(2)]
    SYO = [Ch(nc, es, f"SYO{i}", step=16) for i in range(2)]
    DBG = Ch(nc, es, "DBG", step=16)

    def dump(nm, src_ap, ch, mark):
        if not DEBUG:
            return
        DBG.wait(nc.sync)
        ch.wait(nc.sync, mark)
        DBG.inc(nc.sync.dma_start(dbg[nm].ap(), src_ap))

    def V(emit):
        VE.wait(nc.vector)
        return VE.inc(emit())

    def S(emit):
        SE.wait(nc.scalar)
        return SE.inc(emit())

    def VX(ch, mark=None):
        ch.wait(nc.vector, mark)

    def SX(ch, mark=None):
        ch.wait(nc.scalar, mark)

    def TX(ch, mark=None):
        ch.wait(nc.tensor, mark)

    def sbap(t, off, dims):
        return bass.AP(t.tensor if isinstance(t, bass.AP) else t, off, dims)

    def x_src_ap(k):
        return bass.AP(x_d.ap().tensor, k * NSUP * H,
                       [[H, 128], [128 * H, 4], [1, H]])

    def x_src_half(k, h):
        return bass.AP(x_d.ap().tensor, (k * NSUP + h * 256) * H,
                       [[H, 128], [128 * H, 2], [1, H]])

    def out_dst_ap(k, sub):
        return bass.AP(out_d.ap().tensor, (k * NSUP + sub * NSUB) * H,
                       [[H, 128], [128 * H, 2], [1, H]])

    XEXT = 4 * H
    XSEXT = CH * NSUP

    pp_state = {"i": 0, "hist": [("SE", 0)] * 4}

    def pp_acquire():
        b = pp_state["i"] % 4
        kind, mark = pp_state["hist"][b]
        if kind == "SE":
            TX(SE, mark)
        else:
            TX(VE, mark)
        pp_state["i"] += 1
        return b

    def pp_release(b, kind, mark):
        pp_state["hist"][b] = (kind, mark)

    m_norm_of = {}

    def emit_rot(kk):
        if kk == 0:
            return
        t_a = tmp0[:, 0:NSUB]
        t_b = tmp0[:, NSUB:2 * NSUB]
        for m in range(CH):
            for hf in range(2):
                off = m * NSUP + hf * NSUB
                cseg = cS[:, off:off + NSUB]
                sseg = sS[:, off:off + NSUB]
                V(lambda a=t_a, b=sseg, s2=sD[:, m:m + 1]:
                  nc.vector.tensor_scalar(a, b, s2, None, ALU.mult))
                V(lambda a=t_b, b=cseg, s2=sD[:, m:m + 1]:
                  nc.vector.tensor_scalar(a, b, s2, None, ALU.mult))
                V(lambda a=cseg, s2=cD[:, m:m + 1], t=t_a:
                  nc.vector.scalar_tensor_tensor(
                      a, a, s2, t, ALU.mult, ALU.subtract))
                V(lambda a=sseg, s2=cD[:, m:m + 1], t=t_b:
                  nc.vector.scalar_tensor_tensor(
                      a, a, s2, t, ALU.mult, ALU.add))

    m_veps_of = {}

    def emit_ln_v(kk, h):
        xslk = xsl[kk % 2]
        XS[kk % 2][h].wait(nc.vector)
        for tb in (2 * h, 2 * h + 1):
            seg = xslk[:, tb * H:(tb + 1) * H]
            V(lambda s1=seg: nc.vector.bn_stats(tmp[0][:, 0:6],
                                                s1[:, 0:512]))
            V(lambda s1=seg: nc.vector.bn_stats(tmp[0][:, 6:12],
                                                s1[:, 512:1024]))
            V(lambda t=tb: nc.vector.bn_aggr(
                tmp[0][:, 16 + 2 * t:18 + 2 * t], tmp[0][:, 0:12]))
        vap = sbap(tmp[0], 17 + 4 * h, [[NSUP, 128], [2, 2]])
        veps = carry[:, 16 + 2 * h:18 + 2 * h]
        V(lambda a=veps, b=vap: nc.vector.tensor_scalar(
            a, b, EPS, None, ALU.add))
        m_veps_of[(kk, h)] = VE.n

    m_lnfin = {}
    m_norm23_of = {}

    def emit_ln_sqrt(kk, h):
        map_ = sbap(tmp[0], 16 + 4 * h, [[NSUP, 128], [2, 2]])
        veps = carry[:, 16 + 2 * h:18 + 2 * h]
        inv2 = carry[:, 20 + 2 * h:22 + 2 * h]
        nm2 = carry[:, 24 + 2 * h:26 + 2 * h]
        SX(VE, m_veps_of[(kk, h)])
        m_sq = S(lambda a=veps: nc.scalar.activation(a, a, AF.Sqrt))
        VX(SE, m_sq)
        V(lambda a=inv2, b=veps: nc.vector.reciprocal(a, b))
        V(lambda a=nm2, b=map_, c2=inv2: nc.vector.tensor_tensor(
            a, b, c2, ALU.mult))
        m_lnfin[(kk, h)] = V(lambda a=nm2: nc.vector.tensor_scalar(
            a, a, -1.0, None, ALU.mult))

    def emit_norm01(kk):
        xslk = xsl[kk % 2]
        SX(VE, m_lnfin[(kk, 0)])
        SX(TE, xnT_done_m)
        m_norm = 0
        for tb in range(2):
            m_norm = S(lambda t=tb: nc.scalar.activation(
                xn_tok[:, t * H:(t + 1) * H],
                xslk[:, t * H:(t + 1) * H],
                AF.Identity, scale=carry[:, 20 + t:21 + t],
                bias=carry[:, 24 + t:25 + t]))
        m_norm_of[kk] = m_norm

    def emit_norm23(kk):
        xslk = xsl[kk % 2]
        SX(VE, m_lnfin[(kk, 1)])
        m_norm = 0
        for t in (2, 3):
            m_norm = S(lambda t2=t: nc.scalar.activation(
                g_b[:, (t2 % 2) * H:(t2 % 2) * H + H],
                xslk[:, t2 * H:(t2 + 1) * H],
                AF.Identity, scale=carry[:, 20 + t2:21 + t2],
                bias=carry[:, 24 + t2:25 + t2]))
        m_norm23_of[kk] = m_norm

    chain_done = 0
    sevac_last = 0
    du_last = [0, 0]
    g_ev = [[0] * 8, [0] * 8]
    w1_last = [0, 0]
    xnT_done_m = 0
    out_dma = {}
    st_first_chain = True
    outa_resid_prev = [0, 0]
    rest_last_prev = [0] * CH
    bu_evac_prev = [0] * CH
    st_first_ffn = True

    for src, dst, nch in [(nu_d, tmp0[:, 64:72], CH),
                          (th_d, tmp0[:, 72:80], CH),
                          (gl_d, g_tab[:, :], CH), (dd_d, d_tab[:, :], CH),
                          (b1_d, b1_tab[:, :], CI), (b2_d, b2_tab[:, :], CH)]:
        ap = bass.AP(src.ap().tensor, 0, [[1, 128], [128, nch]])
        SYP.inc(nc.sync.dma_start(dst, ap,
                                  allow_slow_non_contiguous=True))
    m_x = {}
    for sl2 in range(2):
        for h2 in range(2):
            XS[sl2][h2].inc(nc.sync.dma_start(
                sbap(xsl[sl2], h2 * 2 * H, [[XEXT, 128], [H, 2], [1, H]]),
                x_src_half(sl2, h2)))
    m_x[0] = None
    m_x[1] = None

    GP.inc(nc.gpsimd.memset(tmp0[:, 384:512], 1.0))
    GP.wait(nc.gpsimd)
    GP.inc(nc.gpsimd.affine_select(
        ident[:, :], tmp0[:, 384:512], [[1, 128]], ALU.is_equal, 0.0,
        base=0, channel_multiplier=-1))

    SX(SYP)
    S(lambda: nc.scalar.activation(tmp0[:, 96:104], tmp0[:, 64:72], AF.Exp))
    S(lambda: nc.scalar.activation(r_tab[:, :], tmp0[:, 96:104], AF.Exp,
                                   scale=-1.0))
    S(lambda: nc.scalar.activation(tmp0[:, 72:80], tmp0[:, 72:80], AF.Exp))
    S(lambda: nc.scalar.activation(g_tab[:, :], g_tab[:, :], AF.Exp))
    m_gam = SE.n

    VX(SE, m_gam)
    V(lambda: nc.vector.tensor_scalar(tmp0[:, 104:112], tmp0[:, 72:80],
                                      1.0 / TWO_PI, None, ALU.mult))
    V(lambda: nc.vector.tensor_scalar(tmp0[:, 104:112], tmp0[:, 104:112],
                                      RND_M, -RND_M, ALU.add, ALU.add))
    V(lambda: nc.vector.scalar_tensor_tensor(
        tmp0[:, 112:120], tmp0[:, 104:112], -TWO_PI, tmp0[:, 72:80], ALU.mult, ALU.add))
    m_thw = VE.n
    SX(VE, m_thw)
    S(lambda: nc.scalar.activation(tmp0[:, 88:96], tmp0[:, 112:120], AF.Sin))
    V(lambda: nc.vector.tensor_scalar(tmp0[:, 120:128], tmp0[:, 112:120],
                                      PI / 2, None, ALU.add))
    V(lambda: nc.vector.tensor_scalar(tmp0[:, 104:112], tmp0[:, 120:128],
                                      1.0 / TWO_PI, None, ALU.mult))
    V(lambda: nc.vector.tensor_scalar(tmp0[:, 104:112], tmp0[:, 104:112],
                                      RND_M, -RND_M, ALU.add, ALU.add))
    V(lambda: nc.vector.scalar_tensor_tensor(
        tmp0[:, 120:128], tmp0[:, 104:112], -TWO_PI, tmp0[:, 120:128], ALU.mult, ALU.add))
    m_cw = VE.n
    SX(VE, m_cw)
    S(lambda: nc.scalar.activation(tmp0[:, 80:88], tmp0[:, 120:128], AF.Sin))
    m_cs1 = SE.n

    emit_ln_v(0, 0)
    emit_ln_v(0, 1)
    emit_ln_sqrt(0, 0)
    emit_norm01(0)
    emit_ln_sqrt(0, 1)
    emit_norm23(0)
    CEXT = CH * NSUP
    V(lambda: nc.vector.memset(sbap(cS, 0, [[CEXT, 128], [NSUP, CH]]), 1.0))
    V(lambda: nc.vector.memset(sbap(sS, 0, [[CEXT, 128], [NSUP, CH]]), 0.0))
    VX(SE, m_cs1)
    VX(GP)
    V(lambda: nc.vector.tensor_copy(identb[:, :], ident[:, :]))
    m_identb = VE.n
    L = 1
    while L < NSUP:
        for m in range(CH):
            base = m * NSUP
            cseg = cS[:, base:base + L]
            sseg = sS[:, base:base + L]
            cdst = cS[:, base + L:base + 2 * L]
            sdst = sS[:, base + L:base + 2 * L]
            cl = tmp0[:, 80 + m:81 + m]
            sl = tmp0[:, 88 + m:89 + m]
            V(lambda a=tmp0[:, 256:256 + L], b=sseg, s2=sl:
              nc.vector.tensor_scalar(a, b, s2, None, ALU.mult))
            V(lambda a=cdst, b=cseg, s2=cl, c2=tmp0[:, 256:256 + L]:
              nc.vector.scalar_tensor_tensor(a, b, s2, c2,
                                             ALU.mult, ALU.subtract))
            V(lambda a=tmp0[:, 256:256 + L], b=cseg, s2=sl:
              nc.vector.tensor_scalar(a, b, s2, None, ALU.mult))
            V(lambda a=sdst, b=sseg, s2=cl, c2=tmp0[:, 256:256 + L]:
              nc.vector.scalar_tensor_tensor(a, b, s2, c2,
                                             ALU.mult, ALU.add))
        V(lambda: nc.vector.tensor_tensor(tmp0[:, 96:104], tmp0[:, 80:88], tmp0[:, 80:88],
                                          ALU.mult))
        V(lambda: nc.vector.tensor_tensor(tmp0[:, 104:112], tmp0[:, 88:96], tmp0[:, 88:96],
                                          ALU.mult))
        V(lambda: nc.vector.tensor_tensor(tmp0[:, 112:120], tmp0[:, 80:88], tmp0[:, 88:96],
                                          ALU.mult))
        V(lambda: nc.vector.tensor_tensor(tmp0[:, 80:88], tmp0[:, 96:104], tmp0[:, 104:112],
                                          ALU.subtract))
        V(lambda: nc.vector.tensor_scalar(tmp0[:, 88:96], tmp0[:, 112:120], 2.0, None,
                                          ALU.mult))
        L *= 2
    V(lambda: nc.vector.tensor_copy(cD[:, :], tmp0[:, 80:88]))
    V(lambda: nc.vector.tensor_copy(sD[:, :], tmp0[:, 88:96]))
    V(lambda: nc.vector.memset(carry[:, 0:2 * CH], 0.0))
    dump("dbg_cs", cS[:, :], VE, VE.n)
    dump("dbg_ss", sS[:, :], VE, VE.n)
    dump("dbg_rt", r_tab[:, :], SE, SE.n)

    bt_f = [s_b[:, 0:1024], s_b[:, 1024:2048]]
    bt_s = [s_b[:, 2048:3072], s_b[:, 3072:4096]]
    m_sc = [0, 0]
    m_tp = [0, 0]
    m_st = [0, 0]
    prep_it = [0]

    def bc_prep(mat_list, tmo_t, tmo_ext):
        tmo_off = [0, 1024]
        for mat_src, mat_dst, scale in mat_list:
            for i in range(CH):
                pp = prep_it[0] % 2
                if m_sc[pp]:
                    SE.wait(nc.sync, m_sc[pp])
                src_ap = bass.AP(mat_src.ap().tensor, i * 128 * H,
                                 [[H, 128], [1, H]])
                BT[pp].inc(nc.sync.dma_start(bt_f[pp], src_ap))
                SX(BT[pp])
                if m_tp[pp]:
                    SX(TE, m_tp[pp])
                if scale == "g":
                    m_sc[pp] = S(lambda p=pp, ii=i: nc.scalar.activation(
                        bt_s[p], bt_f[p], AF.Identity,
                        scale=g_tab[:, ii:ii + 1]))
                else:
                    m_sc[pp] = S(lambda p=pp, sc2=scale: nc.scalar.activation(
                        bt_s[p], bt_f[p], AF.Identity, scale=float(sc2)))
                TX(SE, m_sc[pp])
                bk0 = pp_acquire()
                bk1 = pp_acquire()
                inst = None
                for j in range(8):
                    bb = bk0 if j < 4 else bk1
                    inst = nc.tensor.transpose(
                        PB[:, bb * 512 + (j % 4) * 128:
                           bb * 512 + (j % 4) * 128 + 128],
                        bt_s[pp][:, j * 128:j * 128 + 128], ident[:, :])
                m_tp[pp] = TE.inc(inst)
                SX(TE, m_tp[pp])
                if m_st[pp]:
                    SX(BST[pp])
                ev0 = S(lambda p=pp, b=bk0: nc.scalar.activation(
                    sbap(tmo_t, tmo_off[p], [[tmo_ext, 128], [1, 512]]),
                    PB[:, b * 512:(b + 1) * 512], AF.Identity))
                ev1 = S(lambda p=pp, b=bk1: nc.scalar.activation(
                    sbap(tmo_t, tmo_off[p] + 512, [[tmo_ext, 128], [1, 512]]),
                    PB[:, b * 512:(b + 1) * 512], AF.Identity))
                pp_release(bk0, "SE", ev0)
                pp_release(bk1, "SE", ev1)
                SE.wait(nc.scalar, ev1)
                dst_ap = bass.AP(mat_dst.ap().tensor, i * 128,
                                 [[H, 128], [128 * H, 8], [1, 128]])
                src2 = sbap(tmo_t, tmo_off[pp],
                            [[tmo_ext, 128], [128, 8], [1, 128]])
                m_st[pp] = BST[pp].inc(nc.scalar.dma_start(dst_ap, src2))
                prep_it[0] += 1

    TX(GP)
    bc_prep([(bre_d, sc_b[0], "g"), (bim_d, sc_b[1], "g")], xsre, XSEXT)

    def w_prep():
        for bch in BST:
            bch.wait(nc.gpsimd)
        wtmp = [h1[:, 0:4096], h1[:, 4096:8192]]
        seen = [False, False]
        for kc in range(CH):
            pp = kc % 2
            if seen[pp]:
                WST[pp].wait(nc.gpsimd)
            src_ap = bass.AP(w1_d.ap().tensor, kc * 128 * I,
                             [[I, 128], [1, I]])
            GDP[pp].inc(nc.gpsimd.dma_start(wtmp[pp], src_ap))
            GDP[pp].wait(nc.sync)
            dst_ap = bass.AP(sc_w1.ap().tensor, kc * 128 * I,
                             [[I, 128], [1, I]])
            WST[pp].inc(nc.sync.dma_start(dst_ap, wtmp[pp]))
            seen[pp] = True
        wtmp2 = [h1[:, 0:1024], h1[:, 1024:2048]]
        for kc in range(CI):
            pp = kc % 2
            WST[pp].wait(nc.gpsimd)
            src_ap = bass.AP(w2_d.ap().tensor, kc * 128 * H,
                             [[H, 128], [1, H]])
            GDP[pp].inc(nc.gpsimd.dma_start(wtmp2[pp], src_ap))
            GDP[pp].wait(nc.sync)
            dst_ap = bass.AP(sc_w2.ap().tensor, kc * 128 * H,
                             [[H, 128], [1, H]])
            WST[pp].inc(nc.sync.dma_start(dst_ap, wtmp2[pp]))

    piece_marks = {}
    piece_use_marks = {}
    pend = []
    n_enq = [0]
    pump_hold = {"cre", "cim", "w1", "w2"}

    def piece_src(kind, q):
        if kind in ("bre", "bim", "cre", "cim"):
            t = {"bre": sc_b[0], "bim": sc_b[1],
                 "cre": sc_c[0], "cim": sc_c[1]}[kind]
            return bass.AP(t.ap().tensor, 0, [[H, 128], [128 * H, 8], [1, H]])
        if kind == "w1":
            return bass.AP(sc_w1.ap().tensor, q * 1024,
                           [[I, 128], [128 * I, 8], [1, 1024]])
        if kind == "w2":
            return bass.AP(sc_w2.ap().tensor, q * 8 * 128 * H,
                           [[H, 128], [128 * H, 8], [1, H]])
        raise ValueError(kind)

    def enq_piece(kind, q=0):
        j = n_enq[0]
        n_enq[0] += 1
        pend.append((j, kind, q))
        piece_use_marks[j] = None
        return j

    def pump():
        while pend:
            j, kind, q = pend[0]
            if kind in pump_hold:
                return
            if j >= 3:
                um = piece_use_marks[j - 3]
                if um is None:
                    return
                nc.sync.wait_ge(TE.h, um)
            piece_marks[j] = WS[j % 3].inc(
                nc.sync.dma_start(wslot[j % 3][:, :], piece_src(kind, q)))
            pend.pop(0)

    def pslot(j):
        return wslot[j % 3]

    def piece_wait(j):
        WS[j % 3].wait(nc.tensor, piece_marks[j])

    def piece_done(j, mark):
        piece_use_marks[j] = mark
        pump()

    for chx in BST:
        chx.wait(nc.sync)
    next_b = [enq_piece("bre"), enq_piece("bim")]
    pump()
    prev_cp = [None, None]
    prev_w = None


    for k in range(NSTEP + 1):
        has_cur = k < NSTEP
        has_prev = k >= 1
        cur_w = prev_w
        cur_cp = prev_cp
        xsl_k = xsl[k % 2]
        xnf_k = xnf[k % 2]
        xnf_p = xnf[(k - 1) % 2]


        if has_prev:
            jc0, jc1 = cur_cp
            pump()
            piece_wait(jc0)
            piece_wait(jc1)
            TX(VE, chain_done)
            last_cx = 0
            for m in range(CH):
                bk = pp_acquire()
                bank = PB[:, bk * 512:bk * 512 + 512]
                first = True
                inst = None
                for cpiece, xsbuf in [(jc0, xsre), (jc1, xsim)]:
                    for kk in range(CH):
                        inst = nc.tensor.matmul(
                            bank,
                            pslot(cpiece)[:, kk * 1024 + m * 128:
                                          kk * 1024 + (m + 1) * 128],
                            xsbuf[:, kk * NSUP:(kk + 1) * NSUP],
                            start=first,
                            stop=(cpiece == jc1 and kk == CH - 1))
                        first = False
                last_cx = TE.inc(inst)
                SX(TE, last_cx)
                sev = S(lambda mm=m, b=bank: nc.scalar.activation(
                    s_b[:, mm * NSUP:(mm + 1) * NSUP], b, AF.Identity))
                pp_release(bk, "SE", sev)
                if m == 5 and has_cur:
                    emit_ln_sqrt(k, 1)
                    emit_norm23(k)
            sevac_last = SE.n
            if DEBUG and k == 1:
                dump("dbg_s", s_b[:, :], VE, sevac_last)
            piece_done(jc0, last_cx)
            piece_done(jc1, last_cx)

        if has_cur:
            TX(SE, m_norm_of[k])
            if k == 0:
                TX(VE, m_identb)
            for sub in range(2):
                if sub == 1:
                    TX(SE, m_norm23_of[k])
                for c in range(CH):
                    bk = pp_acquire()
                    inst = None
                    for tbl in range(2):
                        tb = sub * 2 + tbl
                        dst = PB[:, bk * 512 + tbl * 128:
                                 bk * 512 + tbl * 128 + 128]
                        if sub == 0:
                            inst = nc.tensor.matmul(
                                dst,
                                xn_tok[:, (tb % 2) * H + c * 128:
                                       (tb % 2) * H + (c + 1) * 128],
                                identb[:, :], start=True, stop=True)
                        else:
                            inst = nc.tensor.transpose(
                                dst,
                                g_b[:, (tb % 2) * H + c * 128:
                                    (tb % 2) * H + (c + 1) * 128],
                                ident[:, :])
                    gm = TE.inc(inst)
                    SX(TE, gm)
                    if k >= 2:
                        SX(TE, w1_last[k % 2])
                        SX(VE, du_last[k % 2])
                    ev = S(lambda ss=sub, cc=c, b=bk:
                           nc.scalar.activation(
                               xnf_k[:, cc * NSUP + ss * NSUB:
                                     cc * NSUP + ss * NSUB + NSUB],
                               PB[:, b * 512:b * 512 + 256], AF.Identity))
                    pp_release(bk, "SE", ev)
            xnT_done_m = TE.n
            m_xn_ready = SE.n
            if DEBUG and k == 0:
                dump("dbg_xnf", xnf_k[:, :], SE, m_xn_ready)

        if has_cur:
            jb0, jb1 = next_b
            pump()
            piece_wait(jb0)
            piece_wait(jb1)
            TX(SE, m_xn_ready)
            if st_first_chain:
                SX(BST[0])
                SX(BST[1])
                st_first_chain = False
            bu_evac = [0] * CH
            rest_last = [0] * CH

            def emit_rest(m):
                cseg = cS[:, m * NSUP:(m + 1) * NSUP]
                sseg = sS[:, m * NSUP:(m + 1) * NSUP]
                vrr = vv[m][:, 0:NSUP]
                vii = vv[m][:, NSUP:2 * NSUP]
                yr = tmp[0][:, :]
                yi = tmp[1][:, :]
                rb, vrb = bass.broadcast_tensor_aps(r_tab[:, m:m + 1], vrr)
                V(lambda a=yr, d0=rb, d1=vrb, mm=m:
                  nc.vector.tensor_tensor_scan(
                      a, d0, d1, carry[:, 2 * mm:2 * mm + 1],
                      ALU.mult, ALU.add))
                rb2, vib = bass.broadcast_tensor_aps(r_tab[:, m:m + 1], vii)
                V(lambda a=yi, d0=rb2, d1=vib, mm=m:
                  nc.vector.tensor_tensor_scan(
                      a, d0, d1, carry[:, 2 * mm + 1:2 * mm + 2],
                      ALU.mult, ALU.add))
                V(lambda mm=m, a=yr: nc.vector.tensor_copy(
                    carry[:, 2 * mm:2 * mm + 1], a[:, NSUP - 1:NSUP]))
                V(lambda mm=m, a=yi: nc.vector.tensor_copy(
                    carry[:, 2 * mm + 1:2 * mm + 2], a[:, NSUP - 1:NSUP]))
                xr = xsre[:, m * NSUP:(m + 1) * NSUP]
                xi2 = xsim[:, m * NSUP:(m + 1) * NSUP]
                V(lambda a=xr, b=cseg, c2=yr:
                  nc.vector.tensor_tensor(a, b, c2, ALU.mult))
                V(lambda a=vrr, b=sseg, c2=yi:
                  nc.vector.tensor_tensor(a, b, c2, ALU.mult))
                V(lambda a=xr, b=vrr: nc.vector.tensor_tensor(
                    a, a, b, ALU.subtract))
                V(lambda a=vrr, b=sseg, c2=yr:
                  nc.vector.tensor_tensor(a, b, c2, ALU.mult))
                V(lambda a=vii, b=cseg, c2=yi:
                  nc.vector.tensor_tensor(a, b, c2, ALU.mult))
                rest_last[m] = V(lambda a=xi2, b=vrr, c2=vii:
                                 nc.vector.tensor_tensor(a, b, c2, ALU.add))

            bu_pp = [pp_acquire() for _ in range(4)]
            assert bu_pp == [0, 1, 2, 3], ("rotation misaligned", bu_pp)
            last_bu = 0
            for m in range(CH):
                slot = m % 4
                if slot < 2:
                    pt, base = PB, slot * 1024
                else:
                    pt, base = PA, (slot - 2) * 1024
                if m >= 4:
                    TX(SE, bu_evac[m - 4])
                elif slot >= 2 and k >= 1:
                    TX(SE, g_ev[1][2 * (slot - 2)])
                    TX(SE, g_ev[1][2 * (slot - 2) + 1])
                    TX(VE, outa_resid_prev[slot - 2])
                inst = None
                for comp, bj in [(0, jb0), (1, jb1)]:
                    reg = pt[:, base + comp * 512:base + comp * 512 + 512]
                    for kk in range(CH):
                        inst = nc.tensor.matmul(
                            reg,
                            pslot(bj)[:, kk * 1024 + m * 128:
                                      kk * 1024 + (m + 1) * 128],
                            xnf_k[:, kk * NSUP:(kk + 1) * NSUP],
                            start=(kk == 0), stop=(kk == CH - 1))
                last_bu = TE.inc(inst)

                SX(TE, last_bu)
                SX(VE, rest_last_prev[m])
                bu_evac[m] = S(lambda mm=m, p=pt, b=base:
                               nc.scalar.activation(
                                   vv[mm][:, :], p[:, b:b + 1024],
                                   AF.Identity))

                cseg = cS[:, m * NSUP:(m + 1) * NSUP]
                sseg = sS[:, m * NSUP:(m + 1) * NSUP]
                ur = vv[m][:, 0:NSUP]
                ui = vv[m][:, NSUP:2 * NSUP]
                xr = xsre[:, m * NSUP:(m + 1) * NSUP]
                xi2 = xsim[:, m * NSUP:(m + 1) * NSUP]
                VX(SE, bu_evac[m])
                V(lambda a=xr, b=ur, c2=cseg:
                  nc.vector.tensor_tensor(a, b, c2, ALU.mult))
                V(lambda a=xi2, b=ur, c2=sseg:
                  nc.vector.tensor_tensor(a, b, c2, ALU.mult))
                V(lambda a=ur, b=ui, c2=sseg:
                  nc.vector.tensor_tensor(a, b, c2, ALU.mult))
                V(lambda a=ui, c2=cseg:
                  nc.vector.tensor_tensor(a, a, c2, ALU.mult))
                V(lambda a=ur, b=xr: nc.vector.tensor_tensor(
                    a, b, a, ALU.add))
                V(lambda a=ui, b=xi2: nc.vector.tensor_tensor(
                    a, a, b, ALU.subtract))
                if m >= 2:
                    emit_rest(m - 2)
            if not has_prev:
                emit_rest(CH - 2)
                emit_rest(CH - 1)
                rest_last_prev = list(rest_last)
                if k + 1 < NSTEP:
                    if DEBUG and k == 0:
                        VX(DBG)
                    emit_rot(k + 1)
            else:
                emit_rest_deferred = emit_rest
            bu_evac_prev = bu_evac
            pp_release(bu_pp[0], "SE", bu_evac[4])
            pp_release(bu_pp[1], "SE", bu_evac[4])
            pp_release(bu_pp[2], "SE", bu_evac[5])
            pp_release(bu_pp[3], "SE", bu_evac[5])
            chain_done = VE.n
            if DEBUG and k == 0:
                dump("dbg_xsre", xsre[:, :], VE, chain_done)
                dump("dbg_xsim", xsim[:, :], VE, chain_done)
            piece_done(jb0, last_bu)
            piece_done(jb1, last_bu)
            prev_cp = [enq_piece("cre"), enq_piece("cim")]
            if k + 1 < NSTEP:
                next_b = [enq_piece("bre"), enq_piece("bim")]
            nw1 = [[0] * 4, [0] * 4]
            nw2 = [[0] * 4, [0] * 4]
            for sub in range(2):
                for q in range(4):
                    nw1[sub][q] = enq_piece("w1", q)
                for q in range(4):
                    nw2[sub][q] = enq_piece("w2", q)
            prev_w = (nw1, nw2)
            pump()

        if has_prev:
            new_gev = [[0] * 8, [0] * 8]
            resid_done = [0, 0]
            outa_resid = [0, 0]
            w1p, w2p = cur_w
            for sub in range(2):
                if sub == 0 and st_first_ffn:
                    SX(WST[0])
                    SX(WST[1])
                    st_first_ffn = False
                for q in range(4):
                    jw = w1p[sub][q]
                    pump()
                    piece_wait(jw)
                    inst = None
                    gm = 0
                    for ml in range(8):
                        bk = pp_acquire()
                        bank = PP[bk]
                        for kk in range(CH):
                            inst = nc.tensor.matmul(
                                bank[:, 0:256],
                                pslot(jw)[:, kk * 1024 + ml * 128:
                                          kk * 1024 + (ml + 1) * 128],
                                xnf_p[:, kk * NSUP + sub * NSUB:
                                      kk * NSUP + sub * NSUB + NSUB],
                                start=(kk == 0), stop=(kk == CH - 1))
                        gm = TE.inc(inst)
                        SX(TE, gm)
                        mi = q * 8 + ml
                        ev = S(lambda m2=mi, b=bank: nc.scalar.activation(
                            h1[:, m2 * NSUB:(m2 + 1) * NSUB],
                            b[:, 0:256], AF.Relu,
                            bias=b1_tab[:, m2:m2 + 1]))
                        pp_release(bk, "SE", ev)
                        h1_ev_last = ev
                    piece_done(jw, gm)
                w1_last[(k - 1) % 2] = TE.n
                if DEBUG and k == 1 and sub == 0:
                    dump("dbg_h1", h1[:, :], SE, h1_ev_last)
                if DEBUG and k == 1 and sub == 1:
                    SX(DBG)

                TX(SE, bu_evac_prev[6])
                TX(SE, bu_evac_prev[7])
                if sub == 1:
                    TX(VE, outa_resid[0])
                    TX(VE, outa_resid[1])
                TX(SE, h1_ev_last)
                SX(VE, chain_done)
                ppb = [pp_acquire() for _ in range(4)]
                w2bank = ([PA[:, j * 512:j * 512 + 256] for j in range(4)]
                          + [PB[:, b * 512:b * 512 + 256] for b in ppb])
                for q in range(2):
                    jw = w2p[sub][q]
                    pump()
                    piece_wait(jw)
                    inst = None
                    for m in range(CH):
                        if q == 0 and sub == 1:
                            TX(SE, new_gev[0][m])
                        for kl in range(CH):
                            inst = nc.tensor.matmul(
                                w2bank[m],
                                pslot(jw)[:, kl * 1024 + m * 128:
                                          kl * 1024 + (m + 1) * 128],
                                h1[:, (q * 8 + kl) * NSUB:
                                   (q * 8 + kl + 1) * NSUB],
                                start=(q == 0 and kl == 0),
                                stop=False, skip_group_check=True)
                    gm = TE.inc(inst)
                    piece_done(jw, gm)
                jw2, jw3 = w2p[sub][2], w2p[sub][3]
                pump()
                piece_wait(jw2)
                piece_wait(jw3)
                gm2 = 0
                for m in range(CH):
                    inst = None
                    for q, jw in ((2, jw2), (3, jw3)):
                        for kl in range(CH):
                            inst = nc.tensor.matmul(
                                w2bank[m],
                                pslot(jw)[:, kl * 1024 + m * 128:
                                          kl * 1024 + (m + 1) * 128],
                                h1[:, (q * 8 + kl) * NSUB:
                                   (q * 8 + kl + 1) * NSUB],
                                start=False,
                                stop=(q == 3 and kl == CH - 1),
                                skip_group_check=True)
                    gm2 = TE.inc(inst)
                    SX(TE, gm2)
                    new_gev[sub][m] = S(
                        lambda bnk=w2bank[m], mm=m:
                        nc.scalar.activation(
                            g_b[:, mm * NSUB:(mm + 1) * NSUB], bnk,
                            AF.Silu, bias=b2_tab[:, mm:mm + 1]))
                piece_done(jw2, gm2)
                piece_done(jw3, gm2)
                for jj, b in enumerate(ppb):
                    pp_release(b, "SE", new_gev[sub][4 + jj])

                if DEBUG and k == 1 and sub == 0:
                    dump("dbg_g", g_b[:, :], SE, new_gev[0][CH - 1])
                    VX(DBG)
                tmh = [0, 0]
                for hf in range(2):
                    VX(SE, new_gev[sub][hf * 4 + 3])
                    for mm in range(hf * 4, hf * 4 + 4):
                        V(lambda m2=mm:
                          nc.vector.scalar_tensor_tensor(
                              s_b[:, m2 * NSUP + sub * NSUB:
                                  m2 * NSUP + sub * NSUB + NSUB],
                              xnf_p[:, m2 * NSUP + sub * NSUB:
                                    m2 * NSUP + sub * NSUB + NSUB],
                              d_tab[:, m2:m2 + 1],
                              s_b[:, m2 * NSUP + sub * NSUB:
                                  m2 * NSUP + sub * NSUB + NSUB],
                              ALU.mult, ALU.add))
                    s_ap = sbap(s_b, sub * NSUB + hf * 4 * NSUP,
                                [[CH * NSUP, 128], [NSUP, 4], [1, NSUB]])
                    g_ap = sbap(g_b, hf * 4 * NSUB,
                                [[CH * NSUB, 128], [NSUB, 4], [1, NSUB]])
                    tmh[hf] = V(lambda sa=s_ap, ga=g_ap:
                                nc.vector.tensor_tensor(ga, sa, ga, ALU.mult))

                rm_last = 0
                for halfc in range(2):
                    for tbl in range(2):
                        tb = sub * 2 + tbl
                        if sub == 0:
                            bk2 = 2 * halfc + tbl
                            pdst = PA[:, bk2 * 512:bk2 * 512 + 512]
                            TX(SE, new_gev[0][bk2])
                        else:
                            bk = pp_acquire()
                            pdst = PB[:, bk * 512:bk * 512 + 512]
                        TX(VE, tmh[halfc])
                        inst = None
                        for c4 in range(4):
                            c = halfc * 4 + c4
                            inst = nc.tensor.transpose(
                                pdst[:, c4 * 128:c4 * 128 + 128],
                                g_b[:, c * NSUB + tbl * 128:
                                    c * NSUB + tbl * 128 + 128],
                                ident[:, :])
                        gm = TE.inc(inst)
                        VX(TE, gm)
                        xo = xsl[(k - 1) % 2][:, tb * H + halfc * 512:
                                              tb * H + halfc * 512 + 512]
                        rm = V(lambda a=xo, b2=pdst:
                               nc.vector.tensor_tensor(a, a, b2, ALU.add))
                        if sub == 0:
                            outa_resid[bk2 // 2] = rm
                        else:
                            pp_release(bk, "VE", rm)
                        rm_last = rm

                resid_done[sub] = rm_last
                VE.wait(nc.scalar, rm_last)
                srcx = sbap(xsl[(k - 1) % 2], sub * 2 * H,
                            [[XEXT, 128], [H, 2], [1, H]])
                out_dma[(k - 1, sub)] = SYO[sub].inc(
                    nc.scalar.dma_start(out_dst_ap(k - 1, sub), srcx))
                if k + 1 < NSTEP:
                    sl = (k + 1) % 2
                    SYO[sub].wait(nc.sync)
                    XS[sl][sub].inc(nc.sync.dma_start(
                        sbap(xsl[sl], sub * 2 * H,
                             [[XEXT, 128], [H, 2], [1, H]]),
                        x_src_half(k + 1, sub)))
                if sub == 0 and has_cur:
                    emit_rest_deferred(CH - 2)
                    emit_rest_deferred(CH - 1)
                    rest_last_prev = list(rest_last)
                    chain_done = VE.n
                if k + 1 < NSTEP and sub == 0:
                    emit_rot(k + 1)
                    emit_ln_v(k + 1, 0)
            g_ev = new_gev
            outa_resid_prev = outa_resid
            du_last[(k - 1) % 2] = VE.n
            if k + 1 < NSTEP:
                emit_ln_sqrt(k + 1, 0)
                emit_norm01(k + 1)

        if k == 0:
            w_prep()
            SX(WST[0])
            SX(WST[1])
            bc_prep([(cre_d, sc_c[0], 1.0), (cim_d, sc_c[1], -1.0)],
                    h1, CI * NSUB)
            for chx in BST + WST:
                chx.wait(nc.sync)
            pump_hold.clear()
            pump()
        if k + 1 < NSTEP:
            if not has_prev:
                emit_ln_v(k + 1, 0)
                emit_ln_sqrt(k + 1, 0)
                emit_norm01(k + 1)
            emit_ln_v(k + 1, 1)

    for chx in ([SYP] + SYO + WS + XS[0] + XS[1] + BT + BST + GDP + WST
                + [DBG]):
        chx.wait(nc.sync)
    es.close()
    return nc


_CACHED = {}


def kernel(L0_attn_inputs, L0_attn_nu_log, L0_attn_theta_log, L0_attn_B_re,
           L0_attn_B_im, L0_attn_C_re, L0_attn_C_im, L0_attn_D,
           L0_attn_gamma_log, L0_ffn_W1, L0_ffn_b1, L0_ffn_W2, L0_ffn_b2):
    x = np.ascontiguousarray(np.asarray(L0_attn_inputs, dtype=np.float32))
    common = dict(
        nu_log=np.ascontiguousarray(L0_attn_nu_log, dtype=np.float32),
        theta_log=np.ascontiguousarray(L0_attn_theta_log, dtype=np.float32),
        B_re=np.ascontiguousarray(L0_attn_B_re, dtype=np.float32),
        B_im=np.ascontiguousarray(L0_attn_B_im, dtype=np.float32),
        C_re=np.ascontiguousarray(L0_attn_C_re, dtype=np.float32),
        C_im=np.ascontiguousarray(L0_attn_C_im, dtype=np.float32),
        Dv=np.ascontiguousarray(L0_attn_D, dtype=np.float32),
        gamma_log=np.ascontiguousarray(L0_attn_gamma_log, dtype=np.float32),
        W1=np.ascontiguousarray(L0_ffn_W1, dtype=np.float32),
        b1=np.ascontiguousarray(L0_ffn_b1, dtype=np.float32),
        W2=np.ascontiguousarray(L0_ffn_W2, dtype=np.float32),
        b2=np.ascontiguousarray(L0_ffn_b2, dtype=np.float32),
    )
    if "nc" not in _CACHED:
        _CACHED["nc"] = build_program()
    nc = _CACHED["nc"]
    in_maps = [dict(x=np.ascontiguousarray(x[c]), **common) for c in range(8)]
    res = run_bass_kernel_spmd(nc, in_maps, core_ids=list(range(8)))
    return np.stack([np.asarray(res.results[c]["out"]) for c in range(8)],
                    axis=0)


# version 29
# speedup vs baseline: 1.0511x; 1.0511x over previous
import numpy as np
from contextlib import ExitStack

import concourse.bass as bass
import concourse.mybir as mybir
from concourse.bass_utils import run_bass_kernel_spmd

F32 = mybir.dt.float32
BF16 = mybir.dt.bfloat16
AF = mybir.ActivationFunctionType
ALU = mybir.AluOpType

H = 1024
I = 4096
T = 8192
NSUB = 256
NSUP = 512
NSTEP = T // NSUP
CH = H // 128
CI = I // 128
EPS = 1e-5
PI = float(np.pi)
TWO_PI = float(2 * np.pi)
RND_M = 12582912.0
DEBUG = False


class Ch:
    def __init__(self, nc, es, name, step=1):
        self.h = es.enter_context(nc.semaphore(name))
        self.n = 0
        self.step = step

    def inc(self, inst):
        inst.then_inc(self.h, self.step)
        self.n += self.step
        return self.n

    def wait(self, eng, mark=None):
        v = self.n if mark is None else mark
        if v and v > 0:
            eng.wait_ge(self.h, v)


def build_program():
    nc = bass.Bass("TRN2", target_bir_lowering=False, debug=False)

    x_d = nc.dram_tensor("x", [T, H], F32, kind="ExternalInput")
    nu_d = nc.dram_tensor("nu_log", [H], F32, kind="ExternalInput")
    th_d = nc.dram_tensor("theta_log", [H], F32, kind="ExternalInput")
    bre_d = nc.dram_tensor("B_re", [H, H], F32, kind="ExternalInput")
    bim_d = nc.dram_tensor("B_im", [H, H], F32, kind="ExternalInput")
    cre_d = nc.dram_tensor("C_re", [H, H], F32, kind="ExternalInput")
    cim_d = nc.dram_tensor("C_im", [H, H], F32, kind="ExternalInput")
    dd_d = nc.dram_tensor("Dv", [H], F32, kind="ExternalInput")
    gl_d = nc.dram_tensor("gamma_log", [H], F32, kind="ExternalInput")
    w1_d = nc.dram_tensor("W1", [H, I], F32, kind="ExternalInput")
    b1_d = nc.dram_tensor("b1", [I], F32, kind="ExternalInput")
    w2_d = nc.dram_tensor("W2", [I, H], F32, kind="ExternalInput")
    b2_d = nc.dram_tensor("b2", [H], F32, kind="ExternalInput")
    out_d = nc.dram_tensor("out", [T, H], F32, kind="ExternalOutput")

    sc_b = [nc.dram_tensor(f"sc_b{c}", [H, H], BF16, kind="Internal")
            for c in range(2)]
    sc_c = [nc.dram_tensor(f"sc_c{c}", [H, H], BF16, kind="Internal")
            for c in range(2)]
    sc_w1 = nc.dram_tensor("sc_w1", [H, I], BF16, kind="Internal")
    sc_w2 = nc.dram_tensor("sc_w2", [I, H], BF16, kind="Internal")
    dbg = {}
    if DEBUG:
        for nm, shp, dt in [("dbg_xn", [128, 2048], BF16),
                            ("dbg_xnf", [128, 4096], BF16),
                            ("dbg_xsre", [128, 4096], BF16),
                            ("dbg_xsim", [128, 4096], BF16),
                            ("dbg_s", [128, 4096], F32),
                            ("dbg_h1", [128, 8192], BF16),
                            ("dbg_g", [128, 2048], F32),
                            ("dbg_cs", [128, 4096], F32),
                            ("dbg_ss", [128, 4096], F32),
                            ("dbg_rt", [128, 8], F32)]:
            dbg[nm] = nc.dram_tensor(nm, shp, dt, kind="ExternalOutput")

    es = ExitStack()

    def sb(name, shape, dtype):
        return es.enter_context(nc.sbuf_tensor(name, shape, dtype))

    def psum(name, shape):
        return es.enter_context(nc.psum_tensor(name, shape, F32))

    wslot = [sb(f"wslot{i}", [128, 8 * 1024], BF16) for i in range(3)]
    cS = sb("cS", [128, CH * NSUP], F32)
    sS = sb("sS", [128, CH * NSUP], F32)
    vv = [sb(f"vv{i}", [128, 2 * NSUP], BF16) for i in range(CH)]
    tmp0 = sb("tmp0", [128, NSUP], F32)
    xsre = sb("xsre", [128, CH * NSUP], BF16)
    xsim = sb("xsim", [128, CH * NSUP], BF16)
    s_b = sb("s_b", [128, CH * NSUP], F32)
    g_b = sb("g_b", [128, CH * NSUB], F32)
    tmp = [tmp0, g_b[:, 0:NSUP]]
    h1 = sb("h1", [128, CI * NSUB], BF16)
    xsl = [sb(f"xsl{i}", [128, 4 * H], F32) for i in range(2)]
    xn_tok = sb("xn_tok", [128, 2 * H], BF16)
    xnf = [sb(f"xnf{i}", [128, CH * NSUP], BF16) for i in range(2)]
    carry = sb("carry", [128, 2 * CH + 16], F32)
    r_tab = sb("r_tab", [128, CH], F32)
    d_tab = sb("d_tab", [128, CH], F32)
    g_tab = sb("g_tab", [128, CH], F32)
    b1_tab = sb("b1_tab", [128, CI], F32)
    b2_tab = sb("b2_tab", [128, CH], F32)
    cD = sb("cD", [128, CH], F32)
    sD = sb("sD", [128, CH], F32)
    ident = sb("ident", [128, 128], F32)
    identb = sb("identb", [128, 128], BF16)

    PA = psum("PA", [128, 2048])
    PB = psum("PB", [128, 2048])
    PP = [PB[:, b * 512:(b + 1) * 512] for b in range(4)]

    TE = Ch(nc, es, "TE")
    VE = Ch(nc, es, "VE")
    SE = Ch(nc, es, "SE")
    GP = Ch(nc, es, "GP")
    SYP = Ch(nc, es, "SYP", step=16)
    BT = [Ch(nc, es, f"BT{i}", step=16) for i in range(4)]
    BST = [Ch(nc, es, f"BST{i}", step=16) for i in range(4)]
    GDP = [Ch(nc, es, f"GDP{i}", step=16) for i in range(2)]
    WST = [Ch(nc, es, f"WST{i}", step=16) for i in range(2)]
    WS = [Ch(nc, es, f"WS{i}", step=16) for i in range(3)]
    XS = [[Ch(nc, es, f"XS{i}_{h}", step=16) for h in range(2)]
          for i in range(2)]
    SYO = [Ch(nc, es, f"SYO{i}", step=16) for i in range(2)]
    DBG = Ch(nc, es, "DBG", step=16)

    def dump(nm, src_ap, ch, mark):
        if not DEBUG:
            return
        DBG.wait(nc.sync)
        ch.wait(nc.sync, mark)
        DBG.inc(nc.sync.dma_start(dbg[nm].ap(), src_ap))

    def V(emit):
        VE.wait(nc.vector)
        return VE.inc(emit())

    def S(emit):
        SE.wait(nc.scalar)
        return SE.inc(emit())

    def VX(ch, mark=None):
        ch.wait(nc.vector, mark)

    def SX(ch, mark=None):
        ch.wait(nc.scalar, mark)

    def TX(ch, mark=None):
        ch.wait(nc.tensor, mark)

    def sbap(t, off, dims):
        return bass.AP(t.tensor if isinstance(t, bass.AP) else t, off, dims)

    def x_src_ap(k):
        return bass.AP(x_d.ap().tensor, k * NSUP * H,
                       [[H, 128], [128 * H, 4], [1, H]])

    def x_src_half(k, h):
        return bass.AP(x_d.ap().tensor, (k * NSUP + h * 256) * H,
                       [[H, 128], [128 * H, 2], [1, H]])

    def out_dst_ap(k, sub):
        return bass.AP(out_d.ap().tensor, (k * NSUP + sub * NSUB) * H,
                       [[H, 128], [128 * H, 2], [1, H]])

    XEXT = 4 * H
    XSEXT = CH * NSUP

    pp_state = {"i": 0, "hist": [("SE", 0)] * 4}

    def pp_acquire():
        b = pp_state["i"] % 4
        kind, mark = pp_state["hist"][b]
        if kind == "SE":
            TX(SE, mark)
        else:
            TX(VE, mark)
        pp_state["i"] += 1
        return b

    def pp_release(b, kind, mark):
        pp_state["hist"][b] = (kind, mark)

    m_norm_of = {}

    def emit_rot(kk):
        if kk == 0:
            return
        t_a = tmp0[:, 0:NSUB]
        t_b = tmp0[:, NSUB:2 * NSUB]
        for m in range(CH):
            for hf in range(2):
                off = m * NSUP + hf * NSUB
                cseg = cS[:, off:off + NSUB]
                sseg = sS[:, off:off + NSUB]
                V(lambda a=t_a, b=sseg, s2=sD[:, m:m + 1]:
                  nc.vector.tensor_scalar(a, b, s2, None, ALU.mult))
                V(lambda a=t_b, b=cseg, s2=sD[:, m:m + 1]:
                  nc.vector.tensor_scalar(a, b, s2, None, ALU.mult))
                V(lambda a=cseg, s2=cD[:, m:m + 1], t=t_a:
                  nc.vector.scalar_tensor_tensor(
                      a, a, s2, t, ALU.mult, ALU.subtract))
                V(lambda a=sseg, s2=cD[:, m:m + 1], t=t_b:
                  nc.vector.scalar_tensor_tensor(
                      a, a, s2, t, ALU.mult, ALU.add))

    m_veps_of = {}

    def emit_ln_v(kk, h):
        xslk = xsl[kk % 2]
        XS[kk % 2][h].wait(nc.vector)
        for tb in (2 * h, 2 * h + 1):
            seg = xslk[:, tb * H:(tb + 1) * H]
            V(lambda s1=seg: nc.vector.bn_stats(tmp[0][:, 0:6],
                                                s1[:, 0:512]))
            V(lambda s1=seg: nc.vector.bn_stats(tmp[0][:, 6:12],
                                                s1[:, 512:1024]))
            V(lambda t=tb: nc.vector.bn_aggr(
                tmp[0][:, 16 + 2 * t:18 + 2 * t], tmp[0][:, 0:12]))
        vap = sbap(tmp[0], 17 + 4 * h, [[NSUP, 128], [2, 2]])
        veps = carry[:, 16 + 2 * h:18 + 2 * h]
        V(lambda a=veps, b=vap: nc.vector.tensor_scalar(
            a, b, EPS, None, ALU.add))
        m_veps_of[(kk, h)] = VE.n

    m_lnfin = {}
    m_norm23_of = {}

    def emit_ln_sqrt(kk, h):
        map_ = sbap(tmp[0], 16 + 4 * h, [[NSUP, 128], [2, 2]])
        veps = carry[:, 16 + 2 * h:18 + 2 * h]
        inv2 = carry[:, 20 + 2 * h:22 + 2 * h]
        nm2 = carry[:, 24 + 2 * h:26 + 2 * h]
        SX(VE, m_veps_of[(kk, h)])
        m_sq = S(lambda a=veps: nc.scalar.activation(a, a, AF.Sqrt))
        VX(SE, m_sq)
        V(lambda a=inv2, b=veps: nc.vector.reciprocal(a, b))
        V(lambda a=nm2, b=map_, c2=inv2: nc.vector.tensor_tensor(
            a, b, c2, ALU.mult))
        m_lnfin[(kk, h)] = V(lambda a=nm2: nc.vector.tensor_scalar(
            a, a, -1.0, None, ALU.mult))

    def emit_norm01(kk):
        xslk = xsl[kk % 2]
        SX(VE, m_lnfin[(kk, 0)])
        SX(TE, xnT_done_m)
        m_norm = 0
        for tb in range(2):
            m_norm = S(lambda t=tb: nc.scalar.activation(
                xn_tok[:, t * H:(t + 1) * H],
                xslk[:, t * H:(t + 1) * H],
                AF.Identity, scale=carry[:, 20 + t:21 + t],
                bias=carry[:, 24 + t:25 + t]))
        m_norm_of[kk] = m_norm

    def emit_norm23(kk):
        xslk = xsl[kk % 2]
        SX(VE, m_lnfin[(kk, 1)])
        m_norm = 0
        for t in (2, 3):
            m_norm = S(lambda t2=t: nc.scalar.activation(
                g_b[:, (t2 % 2) * H:(t2 % 2) * H + H],
                xslk[:, t2 * H:(t2 + 1) * H],
                AF.Identity, scale=carry[:, 20 + t2:21 + t2],
                bias=carry[:, 24 + t2:25 + t2]))
        m_norm23_of[kk] = m_norm

    chain_done = 0
    sevac_last = 0
    du_last = [0, 0]
    g_ev = [[0] * 8, [0] * 8]
    w1_last = [0, 0]
    xnT_done_m = 0
    out_dma = {}
    st_first_chain = True
    outa_resid_prev = [0, 0]
    rest_last_prev = [0] * CH
    bu_evac_prev = [0] * CH
    st_first_ffn = True

    for src, dst, nch in [(nu_d, tmp0[:, 64:72], CH),
                          (th_d, tmp0[:, 72:80], CH),
                          (gl_d, g_tab[:, :], CH), (dd_d, d_tab[:, :], CH),
                          (b1_d, b1_tab[:, :], CI), (b2_d, b2_tab[:, :], CH)]:
        ap = bass.AP(src.ap().tensor, 0, [[1, 128], [128, nch]])
        SYP.inc(nc.sync.dma_start(dst, ap,
                                  allow_slow_non_contiguous=True))
    m_x = {}
    for sl2 in range(2):
        for h2 in range(2):
            XS[sl2][h2].inc(nc.sync.dma_start(
                sbap(xsl[sl2], h2 * 2 * H, [[XEXT, 128], [H, 2], [1, H]]),
                x_src_half(sl2, h2)))
    m_x[0] = None
    m_x[1] = None

    GP.inc(nc.gpsimd.memset(tmp0[:, 384:512], 1.0))
    GP.wait(nc.gpsimd)
    GP.inc(nc.gpsimd.affine_select(
        ident[:, :], tmp0[:, 384:512], [[1, 128]], ALU.is_equal, 0.0,
        base=0, channel_multiplier=-1))

    SX(SYP)
    S(lambda: nc.scalar.activation(tmp0[:, 96:104], tmp0[:, 64:72], AF.Exp))
    S(lambda: nc.scalar.activation(r_tab[:, :], tmp0[:, 96:104], AF.Exp,
                                   scale=-1.0))
    S(lambda: nc.scalar.activation(tmp0[:, 72:80], tmp0[:, 72:80], AF.Exp))
    S(lambda: nc.scalar.activation(g_tab[:, :], g_tab[:, :], AF.Exp))
    m_gam = SE.n

    VX(SE, m_gam)
    V(lambda: nc.vector.tensor_scalar(tmp0[:, 104:112], tmp0[:, 72:80],
                                      1.0 / TWO_PI, None, ALU.mult))
    V(lambda: nc.vector.tensor_scalar(tmp0[:, 104:112], tmp0[:, 104:112],
                                      RND_M, -RND_M, ALU.add, ALU.add))
    V(lambda: nc.vector.scalar_tensor_tensor(
        tmp0[:, 112:120], tmp0[:, 104:112], -TWO_PI, tmp0[:, 72:80], ALU.mult, ALU.add))
    m_thw = VE.n
    SX(VE, m_thw)
    S(lambda: nc.scalar.activation(tmp0[:, 88:96], tmp0[:, 112:120], AF.Sin))
    V(lambda: nc.vector.tensor_scalar(tmp0[:, 120:128], tmp0[:, 112:120],
                                      PI / 2, None, ALU.add))
    V(lambda: nc.vector.tensor_scalar(tmp0[:, 104:112], tmp0[:, 120:128],
                                      1.0 / TWO_PI, None, ALU.mult))
    V(lambda: nc.vector.tensor_scalar(tmp0[:, 104:112], tmp0[:, 104:112],
                                      RND_M, -RND_M, ALU.add, ALU.add))
    V(lambda: nc.vector.scalar_tensor_tensor(
        tmp0[:, 120:128], tmp0[:, 104:112], -TWO_PI, tmp0[:, 120:128], ALU.mult, ALU.add))
    m_cw = VE.n
    SX(VE, m_cw)
    S(lambda: nc.scalar.activation(tmp0[:, 80:88], tmp0[:, 120:128], AF.Sin))
    m_cs1 = SE.n

    emit_ln_v(0, 0)
    emit_ln_v(0, 1)
    emit_ln_sqrt(0, 0)
    emit_norm01(0)
    emit_ln_sqrt(0, 1)
    emit_norm23(0)
    CEXT = CH * NSUP
    V(lambda: nc.vector.memset(sbap(cS, 0, [[CEXT, 128], [NSUP, CH]]), 1.0))
    V(lambda: nc.vector.memset(sbap(sS, 0, [[CEXT, 128], [NSUP, CH]]), 0.0))
    VX(SE, m_cs1)
    VX(GP)
    V(lambda: nc.vector.tensor_copy(identb[:, :], ident[:, :]))
    m_identb = VE.n
    L = 1
    while L < NSUP:
        for m in range(CH):
            base = m * NSUP
            cseg = cS[:, base:base + L]
            sseg = sS[:, base:base + L]
            cdst = cS[:, base + L:base + 2 * L]
            sdst = sS[:, base + L:base + 2 * L]
            cl = tmp0[:, 80 + m:81 + m]
            sl = tmp0[:, 88 + m:89 + m]
            V(lambda a=tmp0[:, 256:256 + L], b=sseg, s2=sl:
              nc.vector.tensor_scalar(a, b, s2, None, ALU.mult))
            V(lambda a=cdst, b=cseg, s2=cl, c2=tmp0[:, 256:256 + L]:
              nc.vector.scalar_tensor_tensor(a, b, s2, c2,
                                             ALU.mult, ALU.subtract))
            V(lambda a=tmp0[:, 256:256 + L], b=cseg, s2=sl:
              nc.vector.tensor_scalar(a, b, s2, None, ALU.mult))
            V(lambda a=sdst, b=sseg, s2=cl, c2=tmp0[:, 256:256 + L]:
              nc.vector.scalar_tensor_tensor(a, b, s2, c2,
                                             ALU.mult, ALU.add))
        V(lambda: nc.vector.tensor_tensor(tmp0[:, 96:104], tmp0[:, 80:88], tmp0[:, 80:88],
                                          ALU.mult))
        V(lambda: nc.vector.tensor_tensor(tmp0[:, 104:112], tmp0[:, 88:96], tmp0[:, 88:96],
                                          ALU.mult))
        V(lambda: nc.vector.tensor_tensor(tmp0[:, 112:120], tmp0[:, 80:88], tmp0[:, 88:96],
                                          ALU.mult))
        V(lambda: nc.vector.tensor_tensor(tmp0[:, 80:88], tmp0[:, 96:104], tmp0[:, 104:112],
                                          ALU.subtract))
        V(lambda: nc.vector.tensor_scalar(tmp0[:, 88:96], tmp0[:, 112:120], 2.0, None,
                                          ALU.mult))
        L *= 2
    V(lambda: nc.vector.tensor_copy(cD[:, :], tmp0[:, 80:88]))
    V(lambda: nc.vector.tensor_copy(sD[:, :], tmp0[:, 88:96]))
    V(lambda: nc.vector.memset(carry[:, 0:2 * CH], 0.0))
    dump("dbg_cs", cS[:, :], VE, VE.n)
    dump("dbg_ss", sS[:, :], VE, VE.n)
    dump("dbg_rt", r_tab[:, :], SE, SE.n)

    bt_f = [s_b[:, 0:1024], s_b[:, 1024:2048]]
    bt_s = [s_b[:, 2048:3072], s_b[:, 3072:4096]]
    m_sc = [0, 0]
    m_tp = [0, 0]
    m_st = [0, 0]
    prep_it = [0]

    def bc_prep(mat_list, tmo_t, tmo_ext):
        tmo_off = [0, 1024]
        for mat_src, mat_dst, scale in mat_list:
            for i in range(CH):
                pp = prep_it[0] % 2
                if m_sc[pp]:
                    SE.wait(nc.sync, m_sc[pp])
                src_ap = bass.AP(mat_src.ap().tensor, i * 128 * H,
                                 [[H, 128], [1, H]])
                BT[pp].inc(nc.sync.dma_start(bt_f[pp], src_ap))
                SX(BT[pp])
                if m_tp[pp]:
                    SX(TE, m_tp[pp])
                if scale == "g":
                    m_sc[pp] = S(lambda p=pp, ii=i: nc.scalar.activation(
                        bt_s[p], bt_f[p], AF.Identity,
                        scale=g_tab[:, ii:ii + 1]))
                else:
                    m_sc[pp] = S(lambda p=pp, sc2=scale: nc.scalar.activation(
                        bt_s[p], bt_f[p], AF.Identity, scale=float(sc2)))
                TX(SE, m_sc[pp])
                bk0 = pp_acquire()
                bk1 = pp_acquire()
                inst = None
                for j in range(8):
                    bb = bk0 if j < 4 else bk1
                    inst = nc.tensor.transpose(
                        PB[:, bb * 512 + (j % 4) * 128:
                           bb * 512 + (j % 4) * 128 + 128],
                        bt_s[pp][:, j * 128:j * 128 + 128], ident[:, :])
                m_tp[pp] = TE.inc(inst)
                SX(TE, m_tp[pp])
                if m_st[pp]:
                    SX(BST[pp])
                ev0 = S(lambda p=pp, b=bk0: nc.scalar.activation(
                    sbap(tmo_t, tmo_off[p], [[tmo_ext, 128], [1, 512]]),
                    PB[:, b * 512:(b + 1) * 512], AF.Identity))
                ev1 = S(lambda p=pp, b=bk1: nc.scalar.activation(
                    sbap(tmo_t, tmo_off[p] + 512, [[tmo_ext, 128], [1, 512]]),
                    PB[:, b * 512:(b + 1) * 512], AF.Identity))
                pp_release(bk0, "SE", ev0)
                pp_release(bk1, "SE", ev1)
                SE.wait(nc.scalar, ev1)
                dst_ap = bass.AP(mat_dst.ap().tensor, i * 128,
                                 [[H, 128], [128 * H, 8], [1, 128]])
                src2 = sbap(tmo_t, tmo_off[pp],
                            [[tmo_ext, 128], [128, 8], [1, 128]])
                m_st[pp] = BST[pp].inc(nc.scalar.dma_start(dst_ap, src2))
                prep_it[0] += 1

    TX(GP)
    bc_prep([(bre_d, sc_b[0], "g"), (bim_d, sc_b[1], "g")], xsre, XSEXT)

    def w_prep():
        for bch in BST:
            bch.wait(nc.gpsimd)
        wtmp = [h1[:, 0:4096], h1[:, 4096:8192]]
        seen = [False, False]
        for kc in range(CH):
            pp = kc % 2
            if seen[pp]:
                WST[pp].wait(nc.gpsimd)
            src_ap = bass.AP(w1_d.ap().tensor, kc * 128 * I,
                             [[I, 128], [1, I]])
            GDP[pp].inc(nc.gpsimd.dma_start(wtmp[pp], src_ap))
            GDP[pp].wait(nc.sync)
            dst_ap = bass.AP(sc_w1.ap().tensor, kc * 128 * I,
                             [[I, 128], [1, I]])
            WST[pp].inc(nc.sync.dma_start(dst_ap, wtmp[pp]))
            seen[pp] = True
        wtmp2 = [h1[:, 0:1024], h1[:, 1024:2048]]
        for kc in range(CI):
            pp = kc % 2
            WST[pp].wait(nc.gpsimd)
            src_ap = bass.AP(w2_d.ap().tensor, kc * 128 * H,
                             [[H, 128], [1, H]])
            GDP[pp].inc(nc.gpsimd.dma_start(wtmp2[pp], src_ap))
            GDP[pp].wait(nc.sync)
            dst_ap = bass.AP(sc_w2.ap().tensor, kc * 128 * H,
                             [[H, 128], [1, H]])
            WST[pp].inc(nc.sync.dma_start(dst_ap, wtmp2[pp]))

    piece_marks = {}
    piece_use_marks = {}
    pend = []
    n_enq = [0]
    pump_hold = {"cre", "cim", "w1", "w2"}

    def piece_src(kind, q):
        if kind in ("bre", "bim", "cre", "cim"):
            t = {"bre": sc_b[0], "bim": sc_b[1],
                 "cre": sc_c[0], "cim": sc_c[1]}[kind]
            return bass.AP(t.ap().tensor, 0, [[H, 128], [128 * H, 8], [1, H]])
        if kind == "w1":
            return bass.AP(sc_w1.ap().tensor, q * 1024,
                           [[I, 128], [128 * I, 8], [1, 1024]])
        if kind == "w2":
            return bass.AP(sc_w2.ap().tensor, q * 8 * 128 * H,
                           [[H, 128], [128 * H, 8], [1, H]])
        raise ValueError(kind)

    def enq_piece(kind, q=0):
        j = n_enq[0]
        n_enq[0] += 1
        pend.append((j, kind, q))
        piece_use_marks[j] = None
        return j

    def pump():
        while pend:
            j, kind, q = pend[0]
            if kind in pump_hold:
                return
            if j >= 3:
                um = piece_use_marks[j - 3]
                if um is None:
                    return
                nc.sync.wait_ge(TE.h, um)
            piece_marks[j] = WS[j % 3].inc(
                nc.sync.dma_start(wslot[j % 3][:, :], piece_src(kind, q)))
            pend.pop(0)

    def pslot(j):
        return wslot[j % 3]

    def piece_wait(j):
        WS[j % 3].wait(nc.tensor, piece_marks[j])

    def piece_done(j, mark):
        piece_use_marks[j] = mark
        pump()

    for chx in BST:
        chx.wait(nc.sync)
    next_b = [enq_piece("bre"), enq_piece("bim")]
    pump()
    prev_cp = [None, None]
    prev_w = None


    for k in range(NSTEP + 1):
        has_cur = k < NSTEP
        has_prev = k >= 1
        cur_w = prev_w
        cur_cp = prev_cp
        xsl_k = xsl[k % 2]
        xnf_k = xnf[k % 2]
        xnf_p = xnf[(k - 1) % 2]


        if has_prev:
            jc0, jc1 = cur_cp
            pump()
            piece_wait(jc0)
            piece_wait(jc1)
            TX(VE, chain_done)
            last_cx = 0
            for m in range(CH):
                bk = pp_acquire()
                bank = PB[:, bk * 512:bk * 512 + 512]
                first = True
                inst = None
                for cpiece, xsbuf in [(jc0, xsre), (jc1, xsim)]:
                    for kk in range(CH):
                        inst = nc.tensor.matmul(
                            bank,
                            pslot(cpiece)[:, kk * 1024 + m * 128:
                                          kk * 1024 + (m + 1) * 128],
                            xsbuf[:, kk * NSUP:(kk + 1) * NSUP],
                            start=first,
                            stop=(cpiece == jc1 and kk == CH - 1))
                        first = False
                last_cx = TE.inc(inst)
                SX(TE, last_cx)
                sev = S(lambda mm=m, b=bank: nc.scalar.activation(
                    s_b[:, mm * NSUP:(mm + 1) * NSUP], b, AF.Identity))
                pp_release(bk, "SE", sev)
                if m == 5 and has_cur:
                    emit_ln_sqrt(k, 1)
                    emit_norm23(k)
            sevac_last = SE.n
            if DEBUG and k == 1:
                dump("dbg_s", s_b[:, :], VE, sevac_last)
            piece_done(jc0, last_cx)
            piece_done(jc1, last_cx)

        if has_cur:
            TX(SE, m_norm_of[k])
            if k == 0:
                TX(VE, m_identb)
            for sub in range(2):
                if sub == 1:
                    TX(SE, m_norm23_of[k])
                for c in range(CH):
                    bk = pp_acquire()
                    inst = None
                    for tbl in range(2):
                        tb = sub * 2 + tbl
                        dst = PB[:, bk * 512 + tbl * 128:
                                 bk * 512 + tbl * 128 + 128]
                        if sub == 0:
                            inst = nc.tensor.matmul(
                                dst,
                                xn_tok[:, (tb % 2) * H + c * 128:
                                       (tb % 2) * H + (c + 1) * 128],
                                identb[:, :], start=True, stop=True)
                        else:
                            inst = nc.tensor.transpose(
                                dst,
                                g_b[:, (tb % 2) * H + c * 128:
                                    (tb % 2) * H + (c + 1) * 128],
                                ident[:, :])
                    gm = TE.inc(inst)
                    SX(TE, gm)
                    if k >= 2:
                        SX(TE, w1_last[k % 2])
                        SX(VE, du_last[k % 2])
                    ev = S(lambda ss=sub, cc=c, b=bk:
                           nc.scalar.activation(
                               xnf_k[:, cc * NSUP + ss * NSUB:
                                     cc * NSUP + ss * NSUB + NSUB],
                               PB[:, b * 512:b * 512 + 256], AF.Identity))
                    pp_release(bk, "SE", ev)
            xnT_done_m = TE.n
            m_xn_ready = SE.n
            if DEBUG and k == 0:
                dump("dbg_xnf", xnf_k[:, :], SE, m_xn_ready)

        if has_cur:
            jb0, jb1 = next_b
            pump()
            piece_wait(jb0)
            piece_wait(jb1)
            TX(SE, m_xn_ready)
            if st_first_chain:
                SX(BST[0])
                SX(BST[1])
                st_first_chain = False
            bu_evac = [0] * CH
            rest_last = [0] * CH

            def emit_rest(m):
                cseg = cS[:, m * NSUP:(m + 1) * NSUP]
                sseg = sS[:, m * NSUP:(m + 1) * NSUP]
                vrr = vv[m][:, 0:NSUP]
                vii = vv[m][:, NSUP:2 * NSUP]
                yr = tmp[0][:, :]
                yi = tmp[1][:, :]
                rb, vrb = bass.broadcast_tensor_aps(r_tab[:, m:m + 1], vrr)
                V(lambda a=yr, d0=rb, d1=vrb, mm=m:
                  nc.vector.tensor_tensor_scan(
                      a, d0, d1, carry[:, 2 * mm:2 * mm + 1],
                      ALU.mult, ALU.add))
                rb2, vib = bass.broadcast_tensor_aps(r_tab[:, m:m + 1], vii)
                V(lambda a=yi, d0=rb2, d1=vib, mm=m:
                  nc.vector.tensor_tensor_scan(
                      a, d0, d1, carry[:, 2 * mm + 1:2 * mm + 2],
                      ALU.mult, ALU.add))
                V(lambda mm=m, a=yr: nc.vector.tensor_copy(
                    carry[:, 2 * mm:2 * mm + 1], a[:, NSUP - 1:NSUP]))
                V(lambda mm=m, a=yi: nc.vector.tensor_copy(
                    carry[:, 2 * mm + 1:2 * mm + 2], a[:, NSUP - 1:NSUP]))
                xr = xsre[:, m * NSUP:(m + 1) * NSUP]
                xi2 = xsim[:, m * NSUP:(m + 1) * NSUP]
                V(lambda a=xr, b=cseg, c2=yr:
                  nc.vector.tensor_tensor(a, b, c2, ALU.mult))
                V(lambda a=vrr, b=sseg, c2=yi:
                  nc.vector.tensor_tensor(a, b, c2, ALU.mult))
                V(lambda a=xr, b=vrr: nc.vector.tensor_tensor(
                    a, a, b, ALU.subtract))
                V(lambda a=vrr, b=sseg, c2=yr:
                  nc.vector.tensor_tensor(a, b, c2, ALU.mult))
                V(lambda a=vii, b=cseg, c2=yi:
                  nc.vector.tensor_tensor(a, b, c2, ALU.mult))
                rest_last[m] = V(lambda a=xi2, b=vrr, c2=vii:
                                 nc.vector.tensor_tensor(a, b, c2, ALU.add))

            bu_pp = [pp_acquire() for _ in range(4)]
            assert bu_pp == [0, 1, 2, 3], ("rotation misaligned", bu_pp)
            last_bu = 0
            for m in range(CH):
                slot = m % 4
                if slot < 2:
                    pt, base = PB, slot * 1024
                else:
                    pt, base = PA, (slot - 2) * 1024
                if m >= 4:
                    TX(SE, bu_evac[m - 4])
                elif slot >= 2 and k >= 1:
                    TX(SE, g_ev[1][2 * (slot - 2)])
                    TX(SE, g_ev[1][2 * (slot - 2) + 1])
                    TX(VE, outa_resid_prev[slot - 2])
                inst = None
                for comp, bj in [(0, jb0), (1, jb1)]:
                    reg = pt[:, base + comp * 512:base + comp * 512 + 512]
                    for kk in range(CH):
                        inst = nc.tensor.matmul(
                            reg,
                            pslot(bj)[:, kk * 1024 + m * 128:
                                      kk * 1024 + (m + 1) * 128],
                            xnf_k[:, kk * NSUP:(kk + 1) * NSUP],
                            start=(kk == 0), stop=(kk == CH - 1))
                last_bu = TE.inc(inst)

                SX(TE, last_bu)
                SX(VE, rest_last_prev[m])
                bu_evac[m] = S(lambda mm=m, p=pt, b=base:
                               nc.scalar.activation(
                                   vv[mm][:, :], p[:, b:b + 1024],
                                   AF.Identity))

                cseg = cS[:, m * NSUP:(m + 1) * NSUP]
                sseg = sS[:, m * NSUP:(m + 1) * NSUP]
                ur = vv[m][:, 0:NSUP]
                ui = vv[m][:, NSUP:2 * NSUP]
                xr = xsre[:, m * NSUP:(m + 1) * NSUP]
                xi2 = xsim[:, m * NSUP:(m + 1) * NSUP]
                VX(SE, bu_evac[m])
                V(lambda a=xr, b=ur, c2=cseg:
                  nc.vector.tensor_tensor(a, b, c2, ALU.mult))
                V(lambda a=xi2, b=ur, c2=sseg:
                  nc.vector.tensor_tensor(a, b, c2, ALU.mult))
                V(lambda a=ur, b=ui, c2=sseg:
                  nc.vector.tensor_tensor(a, b, c2, ALU.mult))
                V(lambda a=ui, c2=cseg:
                  nc.vector.tensor_tensor(a, a, c2, ALU.mult))
                V(lambda a=ur, b=xr: nc.vector.tensor_tensor(
                    a, b, a, ALU.add))
                V(lambda a=ui, b=xi2: nc.vector.tensor_tensor(
                    a, a, b, ALU.subtract))
                if m >= 2:
                    emit_rest(m - 2)
            if not has_prev:
                emit_rest(CH - 2)
                emit_rest(CH - 1)
                rest_last_prev = list(rest_last)
                if k + 1 < NSTEP:
                    if DEBUG and k == 0:
                        VX(DBG)
                    emit_rot(k + 1)
            else:
                emit_rest_deferred = emit_rest
            bu_evac_prev = bu_evac
            pp_release(bu_pp[0], "SE", bu_evac[4])
            pp_release(bu_pp[1], "SE", bu_evac[4])
            pp_release(bu_pp[2], "SE", bu_evac[5])
            pp_release(bu_pp[3], "SE", bu_evac[5])
            chain_done = VE.n
            if DEBUG and k == 0:
                dump("dbg_xsre", xsre[:, :], VE, chain_done)
                dump("dbg_xsim", xsim[:, :], VE, chain_done)
            piece_done(jb0, last_bu)
            piece_done(jb1, last_bu)
            prev_cp = [enq_piece("cre"), enq_piece("cim")]
            if k + 1 < NSTEP:
                next_b = [enq_piece("bre"), enq_piece("bim")]
            nw1 = [[0] * 4, [0] * 4]
            nw2 = [[0] * 4, [0] * 4]
            for sub in range(2):
                for q in range(4):
                    nw1[sub][q] = enq_piece("w1", q)
                for q in range(4):
                    nw2[sub][q] = enq_piece("w2", q)
            prev_w = (nw1, nw2)
            pump()

        if has_prev:
            new_gev = [[0] * 8, [0] * 8]
            resid_done = [0, 0]
            outa_resid = [0, 0]
            w1p, w2p = cur_w
            for sub in range(2):
                if sub == 0 and st_first_ffn:
                    SX(WST[0])
                    SX(WST[1])
                    st_first_ffn = False
                for q in range(4):
                    jw = w1p[sub][q]
                    pump()
                    piece_wait(jw)
                    inst = None
                    gm = 0
                    for ml in range(8):
                        bk = pp_acquire()
                        bank = PP[bk]
                        for kk in range(CH):
                            inst = nc.tensor.matmul(
                                bank[:, 0:256],
                                pslot(jw)[:, kk * 1024 + ml * 128:
                                          kk * 1024 + (ml + 1) * 128],
                                xnf_p[:, kk * NSUP + sub * NSUB:
                                      kk * NSUP + sub * NSUB + NSUB],
                                start=(kk == 0), stop=(kk == CH - 1))
                        gm = TE.inc(inst)
                        SX(TE, gm)
                        mi = q * 8 + ml
                        ev = S(lambda m2=mi, b=bank: nc.scalar.activation(
                            h1[:, m2 * NSUB:(m2 + 1) * NSUB],
                            b[:, 0:256], AF.Relu,
                            bias=b1_tab[:, m2:m2 + 1]))
                        pp_release(bk, "SE", ev)
                        h1_ev_last = ev
                    piece_done(jw, gm)
                w1_last[(k - 1) % 2] = TE.n
                if DEBUG and k == 1 and sub == 0:
                    dump("dbg_h1", h1[:, :], SE, h1_ev_last)
                if DEBUG and k == 1 and sub == 1:
                    SX(DBG)

                TX(SE, bu_evac_prev[6])
                TX(SE, bu_evac_prev[7])
                if sub == 1:
                    TX(VE, outa_resid[0])
                    TX(VE, outa_resid[1])
                TX(SE, h1_ev_last)
                SX(VE, chain_done)
                ppb = [pp_acquire() for _ in range(4)]
                w2bank = ([PA[:, j * 512:j * 512 + 256] for j in range(4)]
                          + [PB[:, b * 512:b * 512 + 256] for b in ppb])
                for q in range(4):
                    jw = w2p[sub][q]
                    pump()
                    piece_wait(jw)
                    inst = None
                    gm = 0
                    for m in range(CH):
                        if q == 0 and sub == 1:
                            TX(SE, new_gev[0][m])
                        for kl in range(CH):
                            inst = nc.tensor.matmul(
                                w2bank[m],
                                pslot(jw)[:, kl * 1024 + m * 128:
                                          kl * 1024 + (m + 1) * 128],
                                h1[:, (q * 8 + kl) * NSUB:
                                   (q * 8 + kl + 1) * NSUB],
                                start=(q == 0 and kl == 0),
                                stop=(q == 3 and kl == CH - 1),
                                skip_group_check=True)
                        if q == 3:
                            gm = TE.inc(inst)
                            SX(TE, gm)
                            new_gev[sub][m] = S(
                                lambda bnk=w2bank[m], mm=m:
                                nc.scalar.activation(
                                    g_b[:, mm * NSUB:(mm + 1) * NSUB], bnk,
                                    AF.Silu, bias=b2_tab[:, mm:mm + 1]))
                    if q < 3:
                        gm = TE.inc(inst)
                    piece_done(jw, gm)
                for jj, b in enumerate(ppb):
                    pp_release(b, "SE", new_gev[sub][4 + jj])

                if DEBUG and k == 1 and sub == 0:
                    dump("dbg_g", g_b[:, :], SE, new_gev[0][CH - 1])
                    VX(DBG)
                tmh = [0, 0]
                for hf in range(2):
                    VX(SE, new_gev[sub][hf * 4 + 3])
                    for mm in range(hf * 4, hf * 4 + 4):
                        V(lambda m2=mm:
                          nc.vector.scalar_tensor_tensor(
                              s_b[:, m2 * NSUP + sub * NSUB:
                                  m2 * NSUP + sub * NSUB + NSUB],
                              xnf_p[:, m2 * NSUP + sub * NSUB:
                                    m2 * NSUP + sub * NSUB + NSUB],
                              d_tab[:, m2:m2 + 1],
                              s_b[:, m2 * NSUP + sub * NSUB:
                                  m2 * NSUP + sub * NSUB + NSUB],
                              ALU.mult, ALU.add))
                    s_ap = sbap(s_b, sub * NSUB + hf * 4 * NSUP,
                                [[CH * NSUP, 128], [NSUP, 4], [1, NSUB]])
                    g_ap = sbap(g_b, hf * 4 * NSUB,
                                [[CH * NSUB, 128], [NSUB, 4], [1, NSUB]])
                    tmh[hf] = V(lambda sa=s_ap, ga=g_ap:
                                nc.vector.tensor_tensor(ga, sa, ga, ALU.mult))

                rm_last = 0
                for halfc in range(2):
                    for tbl in range(2):
                        tb = sub * 2 + tbl
                        if sub == 0:
                            bk2 = 2 * halfc + tbl
                            pdst = PA[:, bk2 * 512:bk2 * 512 + 512]
                            TX(SE, new_gev[0][bk2])
                        else:
                            bk = pp_acquire()
                            pdst = PB[:, bk * 512:bk * 512 + 512]
                        TX(VE, tmh[halfc])
                        inst = None
                        for c4 in range(4):
                            c = halfc * 4 + c4
                            inst = nc.tensor.transpose(
                                pdst[:, c4 * 128:c4 * 128 + 128],
                                g_b[:, c * NSUB + tbl * 128:
                                    c * NSUB + tbl * 128 + 128],
                                ident[:, :])
                        gm = TE.inc(inst)
                        VX(TE, gm)
                        xo = xsl[(k - 1) % 2][:, tb * H + halfc * 512:
                                              tb * H + halfc * 512 + 512]
                        rm = V(lambda a=xo, b2=pdst:
                               nc.vector.tensor_tensor(a, a, b2, ALU.add))
                        if sub == 0:
                            outa_resid[bk2 // 2] = rm
                        else:
                            pp_release(bk, "VE", rm)
                        rm_last = rm

                resid_done[sub] = rm_last
                VE.wait(nc.scalar, rm_last)
                srcx = sbap(xsl[(k - 1) % 2], sub * 2 * H,
                            [[XEXT, 128], [H, 2], [1, H]])
                out_dma[(k - 1, sub)] = SYO[sub].inc(
                    nc.scalar.dma_start(out_dst_ap(k - 1, sub), srcx))
                if k + 1 < NSTEP:
                    sl = (k + 1) % 2
                    SYO[sub].wait(nc.sync)
                    XS[sl][sub].inc(nc.sync.dma_start(
                        sbap(xsl[sl], sub * 2 * H,
                             [[XEXT, 128], [H, 2], [1, H]]),
                        x_src_half(k + 1, sub)))
                if sub == 0 and has_cur:
                    emit_rest_deferred(CH - 2)
                    emit_rest_deferred(CH - 1)
                    rest_last_prev = list(rest_last)
                    chain_done = VE.n
                if k + 1 < NSTEP and sub == 0:
                    emit_rot(k + 1)
                    emit_ln_v(k + 1, 0)
            g_ev = new_gev
            outa_resid_prev = outa_resid
            du_last[(k - 1) % 2] = VE.n
            if k + 1 < NSTEP:
                emit_ln_sqrt(k + 1, 0)
                emit_norm01(k + 1)

        if k == 0:
            bc_prep([(cre_d, sc_c[0], 1.0), (cim_d, sc_c[1], -1.0)],
                    h1, CI * NSUB)
            for chx in BST:
                chx.wait(nc.sync)
            pump_hold.discard("cre")
            pump_hold.discard("cim")
            pump()
            w_prep()
            for chx in WST:
                chx.wait(nc.sync)
            pump_hold.clear()
            pump()
        if k + 1 < NSTEP:
            if not has_prev:
                emit_ln_v(k + 1, 0)
                emit_ln_sqrt(k + 1, 0)
                emit_norm01(k + 1)
            emit_ln_v(k + 1, 1)

    for chx in ([SYP] + SYO + WS + XS[0] + XS[1] + BT + BST + GDP + WST
                + [DBG]):
        chx.wait(nc.sync)
    es.close()
    return nc


_CACHED = {}


def kernel(L0_attn_inputs, L0_attn_nu_log, L0_attn_theta_log, L0_attn_B_re,
           L0_attn_B_im, L0_attn_C_re, L0_attn_C_im, L0_attn_D,
           L0_attn_gamma_log, L0_ffn_W1, L0_ffn_b1, L0_ffn_W2, L0_ffn_b2):
    x = np.ascontiguousarray(np.asarray(L0_attn_inputs, dtype=np.float32))
    common = dict(
        nu_log=np.ascontiguousarray(L0_attn_nu_log, dtype=np.float32),
        theta_log=np.ascontiguousarray(L0_attn_theta_log, dtype=np.float32),
        B_re=np.ascontiguousarray(L0_attn_B_re, dtype=np.float32),
        B_im=np.ascontiguousarray(L0_attn_B_im, dtype=np.float32),
        C_re=np.ascontiguousarray(L0_attn_C_re, dtype=np.float32),
        C_im=np.ascontiguousarray(L0_attn_C_im, dtype=np.float32),
        Dv=np.ascontiguousarray(L0_attn_D, dtype=np.float32),
        gamma_log=np.ascontiguousarray(L0_attn_gamma_log, dtype=np.float32),
        W1=np.ascontiguousarray(L0_ffn_W1, dtype=np.float32),
        b1=np.ascontiguousarray(L0_ffn_b1, dtype=np.float32),
        W2=np.ascontiguousarray(L0_ffn_W2, dtype=np.float32),
        b2=np.ascontiguousarray(L0_ffn_b2, dtype=np.float32),
    )
    if "nc" not in _CACHED:
        _CACHED["nc"] = build_program()
    nc = _CACHED["nc"]
    in_maps = [dict(x=np.ascontiguousarray(x[c]), **common) for c in range(8)]
    res = run_bass_kernel_spmd(nc, in_maps, core_ids=list(range(8)))
    return np.stack([np.asarray(res.results[c]["out"]) for c in range(8)],
                    axis=0)


# version 30 (final)
# speedup vs baseline: 1.0596x; 1.0080x over previous
import numpy as np
from contextlib import ExitStack

import concourse.bass as bass
import concourse.mybir as mybir
from concourse.bass_utils import run_bass_kernel_spmd

F32 = mybir.dt.float32
BF16 = mybir.dt.bfloat16
AF = mybir.ActivationFunctionType
ALU = mybir.AluOpType

H = 1024
I = 4096
T = 8192
NSUB = 256
NSUP = 512
NSTEP = T // NSUP
CH = H // 128
CI = I // 128
EPS = 1e-5
PI = float(np.pi)
TWO_PI = float(2 * np.pi)
RND_M = 12582912.0
DEBUG = False


class Ch:
    def __init__(self, nc, es, name, step=1):
        self.h = es.enter_context(nc.semaphore(name))
        self.n = 0
        self.step = step

    def inc(self, inst):
        inst.then_inc(self.h, self.step)
        self.n += self.step
        return self.n

    def wait(self, eng, mark=None):
        v = self.n if mark is None else mark
        if v and v > 0:
            eng.wait_ge(self.h, v)


def build_program():
    nc = bass.Bass("TRN2", target_bir_lowering=False, debug=False)

    x_d = nc.dram_tensor("x", [T, H], F32, kind="ExternalInput")
    nu_d = nc.dram_tensor("nu_log", [H], F32, kind="ExternalInput")
    th_d = nc.dram_tensor("theta_log", [H], F32, kind="ExternalInput")
    bre_d = nc.dram_tensor("B_re", [H, H], F32, kind="ExternalInput")
    bim_d = nc.dram_tensor("B_im", [H, H], F32, kind="ExternalInput")
    cre_d = nc.dram_tensor("C_re", [H, H], F32, kind="ExternalInput")
    cim_d = nc.dram_tensor("C_im", [H, H], F32, kind="ExternalInput")
    dd_d = nc.dram_tensor("Dv", [H], F32, kind="ExternalInput")
    gl_d = nc.dram_tensor("gamma_log", [H], F32, kind="ExternalInput")
    w1_d = nc.dram_tensor("W1", [H, I], F32, kind="ExternalInput")
    b1_d = nc.dram_tensor("b1", [I], F32, kind="ExternalInput")
    w2_d = nc.dram_tensor("W2", [I, H], F32, kind="ExternalInput")
    b2_d = nc.dram_tensor("b2", [H], F32, kind="ExternalInput")
    out_d = nc.dram_tensor("out", [T, H], F32, kind="ExternalOutput")

    sc_b = [nc.dram_tensor(f"sc_b{c}", [H, H], BF16, kind="Internal")
            for c in range(2)]
    sc_c = [nc.dram_tensor(f"sc_c{c}", [H, H], BF16, kind="Internal")
            for c in range(2)]
    sc_w1 = nc.dram_tensor("sc_w1", [H, I], BF16, kind="Internal")
    sc_w2 = nc.dram_tensor("sc_w2", [I, H], BF16, kind="Internal")
    dbg = {}
    if DEBUG:
        for nm, shp, dt in [("dbg_xn", [128, 2048], BF16),
                            ("dbg_xnf", [128, 4096], BF16),
                            ("dbg_xsre", [128, 4096], BF16),
                            ("dbg_xsim", [128, 4096], BF16),
                            ("dbg_s", [128, 4096], F32),
                            ("dbg_h1", [128, 8192], BF16),
                            ("dbg_g", [128, 2048], F32),
                            ("dbg_cs", [128, 4096], F32),
                            ("dbg_ss", [128, 4096], F32),
                            ("dbg_rt", [128, 8], F32)]:
            dbg[nm] = nc.dram_tensor(nm, shp, dt, kind="ExternalOutput")

    es = ExitStack()

    def sb(name, shape, dtype):
        return es.enter_context(nc.sbuf_tensor(name, shape, dtype))

    def psum(name, shape):
        return es.enter_context(nc.psum_tensor(name, shape, F32))

    wslot = [sb(f"wslot{i}", [128, 8 * 1024], BF16) for i in range(3)]
    cS = sb("cS", [128, CH * NSUP], F32)
    sS = sb("sS", [128, CH * NSUP], F32)
    vv = [sb(f"vv{i}", [128, 2 * NSUP], BF16) for i in range(CH)]
    tmp0 = sb("tmp0", [128, NSUP], F32)
    xsre = sb("xsre", [128, CH * NSUP], BF16)
    xsim = sb("xsim", [128, CH * NSUP], BF16)
    s_b = sb("s_b", [128, CH * NSUP], F32)
    g_b = sb("g_b", [128, CH * NSUB], F32)
    tmp = [tmp0, g_b[:, 0:NSUP]]
    h1 = sb("h1", [128, CI * NSUB], BF16)
    xsl = [sb(f"xsl{i}", [128, 4 * H], F32) for i in range(2)]
    xn_tok = sb("xn_tok", [128, 2 * H], BF16)
    xnf = [sb(f"xnf{i}", [128, CH * NSUP], BF16) for i in range(2)]
    carry = sb("carry", [128, 2 * CH + 16], F32)
    r_tab = sb("r_tab", [128, CH], F32)
    d_tab = sb("d_tab", [128, CH], F32)
    g_tab = sb("g_tab", [128, CH], F32)
    b1_tab = sb("b1_tab", [128, CI], F32)
    b2_tab = sb("b2_tab", [128, CH], F32)
    cD = sb("cD", [128, CH], F32)
    sD = sb("sD", [128, CH], F32)
    ident = sb("ident", [128, 128], F32)
    identb = sb("identb", [128, 128], BF16)

    PA = psum("PA", [128, 2048])
    PB = psum("PB", [128, 2048])
    PP = [PB[:, b * 512:(b + 1) * 512] for b in range(4)]

    TE = Ch(nc, es, "TE")
    VE = Ch(nc, es, "VE")
    SE = Ch(nc, es, "SE")
    GP = Ch(nc, es, "GP")
    SYP = Ch(nc, es, "SYP", step=16)
    BT = [Ch(nc, es, f"BT{i}", step=16) for i in range(4)]
    BST = [Ch(nc, es, f"BST{i}", step=16) for i in range(4)]
    GDP = [Ch(nc, es, f"GDP{i}", step=16) for i in range(2)]
    WST = [Ch(nc, es, f"WST{i}", step=16) for i in range(2)]
    WS = [Ch(nc, es, f"WS{i}", step=16) for i in range(3)]
    XS = [[Ch(nc, es, f"XS{i}_{h}", step=16) for h in range(2)]
          for i in range(2)]
    SYO = [Ch(nc, es, f"SYO{i}", step=16) for i in range(2)]
    DBG = Ch(nc, es, "DBG", step=16)

    def dump(nm, src_ap, ch, mark):
        if not DEBUG:
            return
        DBG.wait(nc.sync)
        ch.wait(nc.sync, mark)
        DBG.inc(nc.sync.dma_start(dbg[nm].ap(), src_ap))

    def V(emit):
        VE.wait(nc.vector)
        return VE.inc(emit())

    def S(emit):
        SE.wait(nc.scalar)
        return SE.inc(emit())

    def VX(ch, mark=None):
        ch.wait(nc.vector, mark)

    def SX(ch, mark=None):
        ch.wait(nc.scalar, mark)

    def TX(ch, mark=None):
        ch.wait(nc.tensor, mark)

    def sbap(t, off, dims):
        return bass.AP(t.tensor if isinstance(t, bass.AP) else t, off, dims)

    def x_src_ap(k):
        return bass.AP(x_d.ap().tensor, k * NSUP * H,
                       [[H, 128], [128 * H, 4], [1, H]])

    def x_src_half(k, h):
        return bass.AP(x_d.ap().tensor, (k * NSUP + h * 256) * H,
                       [[H, 128], [128 * H, 2], [1, H]])

    def out_dst_ap(k, sub):
        return bass.AP(out_d.ap().tensor, (k * NSUP + sub * NSUB) * H,
                       [[H, 128], [128 * H, 2], [1, H]])

    XEXT = 4 * H
    XSEXT = CH * NSUP

    pp_state = {"i": 0, "hist": [("SE", 0)] * 4}

    def pp_acquire():
        b = pp_state["i"] % 4
        kind, mark = pp_state["hist"][b]
        if kind == "SE":
            TX(SE, mark)
        else:
            TX(VE, mark)
        pp_state["i"] += 1
        return b

    def pp_release(b, kind, mark):
        pp_state["hist"][b] = (kind, mark)

    m_norm_of = {}

    def emit_rot(kk):
        if kk == 0:
            return
        t_a = tmp0[:, 0:NSUB]
        t_b = tmp0[:, NSUB:2 * NSUB]
        for m in range(CH):
            for hf in range(2):
                off = m * NSUP + hf * NSUB
                cseg = cS[:, off:off + NSUB]
                sseg = sS[:, off:off + NSUB]
                V(lambda a=t_a, b=sseg, s2=sD[:, m:m + 1]:
                  nc.vector.tensor_scalar(a, b, s2, None, ALU.mult))
                V(lambda a=t_b, b=cseg, s2=sD[:, m:m + 1]:
                  nc.vector.tensor_scalar(a, b, s2, None, ALU.mult))
                V(lambda a=cseg, s2=cD[:, m:m + 1], t=t_a:
                  nc.vector.scalar_tensor_tensor(
                      a, a, s2, t, ALU.mult, ALU.subtract))
                V(lambda a=sseg, s2=cD[:, m:m + 1], t=t_b:
                  nc.vector.scalar_tensor_tensor(
                      a, a, s2, t, ALU.mult, ALU.add))

    m_veps_of = {}

    def emit_ln_v(kk, h):
        xslk = xsl[kk % 2]
        XS[kk % 2][h].wait(nc.vector)
        for tb in (2 * h, 2 * h + 1):
            seg = xslk[:, tb * H:(tb + 1) * H]
            V(lambda s1=seg: nc.vector.bn_stats(tmp[0][:, 0:6],
                                                s1[:, 0:512]))
            V(lambda s1=seg: nc.vector.bn_stats(tmp[0][:, 6:12],
                                                s1[:, 512:1024]))
            V(lambda t=tb: nc.vector.bn_aggr(
                tmp[0][:, 16 + 2 * t:18 + 2 * t], tmp[0][:, 0:12]))
        vap = sbap(tmp[0], 17 + 4 * h, [[NSUP, 128], [2, 2]])
        veps = carry[:, 16 + 2 * h:18 + 2 * h]
        V(lambda a=veps, b=vap: nc.vector.tensor_scalar(
            a, b, EPS, None, ALU.add))
        m_veps_of[(kk, h)] = VE.n

    m_lnfin = {}
    m_norm23_of = {}

    def emit_ln_sqrt(kk, h):
        map_ = sbap(tmp[0], 16 + 4 * h, [[NSUP, 128], [2, 2]])
        veps = carry[:, 16 + 2 * h:18 + 2 * h]
        inv2 = carry[:, 20 + 2 * h:22 + 2 * h]
        nm2 = carry[:, 24 + 2 * h:26 + 2 * h]
        SX(VE, m_veps_of[(kk, h)])
        m_sq = S(lambda a=veps: nc.scalar.activation(a, a, AF.Sqrt))
        VX(SE, m_sq)
        V(lambda a=inv2, b=veps: nc.vector.reciprocal(a, b))
        V(lambda a=nm2, b=map_, c2=inv2: nc.vector.tensor_tensor(
            a, b, c2, ALU.mult))
        m_lnfin[(kk, h)] = V(lambda a=nm2: nc.vector.tensor_scalar(
            a, a, -1.0, None, ALU.mult))

    def emit_norm01(kk):
        xslk = xsl[kk % 2]
        SX(VE, m_lnfin[(kk, 0)])
        SX(TE, xnT_done_m)
        m_norm = 0
        for tb in range(2):
            m_norm = S(lambda t=tb: nc.scalar.activation(
                xn_tok[:, t * H:(t + 1) * H],
                xslk[:, t * H:(t + 1) * H],
                AF.Identity, scale=carry[:, 20 + t:21 + t],
                bias=carry[:, 24 + t:25 + t]))
        m_norm_of[kk] = m_norm

    def emit_norm23(kk):
        xslk = xsl[kk % 2]
        SX(VE, m_lnfin[(kk, 1)])
        m_norm = 0
        for t in (2, 3):
            m_norm = S(lambda t2=t: nc.scalar.activation(
                g_b[:, (t2 % 2) * H:(t2 % 2) * H + H],
                xslk[:, t2 * H:(t2 + 1) * H],
                AF.Identity, scale=carry[:, 20 + t2:21 + t2],
                bias=carry[:, 24 + t2:25 + t2]))
        m_norm23_of[kk] = m_norm

    chain_done = 0
    sevac_last = 0
    du_last = [0, 0]
    g_ev = [[0] * 8, [0] * 8]
    w1_last = [0, 0]
    xnT_done_m = 0
    out_dma = {}
    st_first_chain = True
    outa_resid_prev = [0, 0]
    rest_last_prev = [0] * CH
    bu_evac_prev = [0] * CH
    st_first_ffn = True

    for src, dst, nch in [(nu_d, tmp0[:, 64:72], CH),
                          (th_d, tmp0[:, 72:80], CH),
                          (gl_d, g_tab[:, :], CH), (dd_d, d_tab[:, :], CH),
                          (b1_d, b1_tab[:, :], CI), (b2_d, b2_tab[:, :], CH)]:
        ap = bass.AP(src.ap().tensor, 0, [[1, 128], [128, nch]])
        SYP.inc(nc.sync.dma_start(dst, ap,
                                  allow_slow_non_contiguous=True))
    m_x = {}
    for sl2 in range(2):
        for h2 in range(2):
            XS[sl2][h2].inc(nc.sync.dma_start(
                sbap(xsl[sl2], h2 * 2 * H, [[XEXT, 128], [H, 2], [1, H]]),
                x_src_half(sl2, h2)))
    m_x[0] = None
    m_x[1] = None

    GP.inc(nc.gpsimd.memset(tmp0[:, 384:512], 1.0))
    GP.wait(nc.gpsimd)
    GP.inc(nc.gpsimd.affine_select(
        ident[:, :], tmp0[:, 384:512], [[1, 128]], ALU.is_equal, 0.0,
        base=0, channel_multiplier=-1))

    SX(SYP)
    S(lambda: nc.scalar.activation(tmp0[:, 96:104], tmp0[:, 64:72], AF.Exp))
    S(lambda: nc.scalar.activation(r_tab[:, :], tmp0[:, 96:104], AF.Exp,
                                   scale=-1.0))
    S(lambda: nc.scalar.activation(tmp0[:, 72:80], tmp0[:, 72:80], AF.Exp))
    S(lambda: nc.scalar.activation(g_tab[:, :], g_tab[:, :], AF.Exp))
    m_gam = SE.n

    VX(SE, m_gam)
    V(lambda: nc.vector.tensor_scalar(tmp0[:, 104:112], tmp0[:, 72:80],
                                      1.0 / TWO_PI, None, ALU.mult))
    V(lambda: nc.vector.tensor_scalar(tmp0[:, 104:112], tmp0[:, 104:112],
                                      RND_M, -RND_M, ALU.add, ALU.add))
    V(lambda: nc.vector.scalar_tensor_tensor(
        tmp0[:, 112:120], tmp0[:, 104:112], -TWO_PI, tmp0[:, 72:80], ALU.mult, ALU.add))
    m_thw = VE.n
    SX(VE, m_thw)
    S(lambda: nc.scalar.activation(tmp0[:, 88:96], tmp0[:, 112:120], AF.Sin))
    V(lambda: nc.vector.tensor_scalar(tmp0[:, 120:128], tmp0[:, 112:120],
                                      PI / 2, None, ALU.add))
    V(lambda: nc.vector.tensor_scalar(tmp0[:, 104:112], tmp0[:, 120:128],
                                      1.0 / TWO_PI, None, ALU.mult))
    V(lambda: nc.vector.tensor_scalar(tmp0[:, 104:112], tmp0[:, 104:112],
                                      RND_M, -RND_M, ALU.add, ALU.add))
    V(lambda: nc.vector.scalar_tensor_tensor(
        tmp0[:, 120:128], tmp0[:, 104:112], -TWO_PI, tmp0[:, 120:128], ALU.mult, ALU.add))
    m_cw = VE.n
    SX(VE, m_cw)
    S(lambda: nc.scalar.activation(tmp0[:, 80:88], tmp0[:, 120:128], AF.Sin))
    m_cs1 = SE.n

    emit_ln_v(0, 0)
    emit_ln_v(0, 1)
    emit_ln_sqrt(0, 0)
    emit_norm01(0)
    emit_ln_sqrt(0, 1)
    emit_norm23(0)
    CEXT = CH * NSUP
    V(lambda: nc.vector.memset(sbap(cS, 0, [[CEXT, 128], [NSUP, CH]]), 1.0))
    V(lambda: nc.vector.memset(sbap(sS, 0, [[CEXT, 128], [NSUP, CH]]), 0.0))
    VX(SE, m_cs1)
    VX(GP)
    V(lambda: nc.vector.tensor_copy(identb[:, :], ident[:, :]))
    m_identb = VE.n
    L = 1
    while L < NSUP:
        for m in range(CH):
            base = m * NSUP
            cseg = cS[:, base:base + L]
            sseg = sS[:, base:base + L]
            cdst = cS[:, base + L:base + 2 * L]
            sdst = sS[:, base + L:base + 2 * L]
            cl = tmp0[:, 80 + m:81 + m]
            sl = tmp0[:, 88 + m:89 + m]
            V(lambda a=tmp0[:, 256:256 + L], b=sseg, s2=sl:
              nc.vector.tensor_scalar(a, b, s2, None, ALU.mult))
            V(lambda a=cdst, b=cseg, s2=cl, c2=tmp0[:, 256:256 + L]:
              nc.vector.scalar_tensor_tensor(a, b, s2, c2,
                                             ALU.mult, ALU.subtract))
            V(lambda a=tmp0[:, 256:256 + L], b=cseg, s2=sl:
              nc.vector.tensor_scalar(a, b, s2, None, ALU.mult))
            V(lambda a=sdst, b=sseg, s2=cl, c2=tmp0[:, 256:256 + L]:
              nc.vector.scalar_tensor_tensor(a, b, s2, c2,
                                             ALU.mult, ALU.add))
        V(lambda: nc.vector.tensor_tensor(tmp0[:, 96:104], tmp0[:, 80:88], tmp0[:, 80:88],
                                          ALU.mult))
        V(lambda: nc.vector.tensor_tensor(tmp0[:, 104:112], tmp0[:, 88:96], tmp0[:, 88:96],
                                          ALU.mult))
        V(lambda: nc.vector.tensor_tensor(tmp0[:, 112:120], tmp0[:, 80:88], tmp0[:, 88:96],
                                          ALU.mult))
        V(lambda: nc.vector.tensor_tensor(tmp0[:, 80:88], tmp0[:, 96:104], tmp0[:, 104:112],
                                          ALU.subtract))
        V(lambda: nc.vector.tensor_scalar(tmp0[:, 88:96], tmp0[:, 112:120], 2.0, None,
                                          ALU.mult))
        L *= 2
    V(lambda: nc.vector.tensor_copy(cD[:, :], tmp0[:, 80:88]))
    V(lambda: nc.vector.tensor_copy(sD[:, :], tmp0[:, 88:96]))
    V(lambda: nc.vector.memset(carry[:, 0:2 * CH], 0.0))
    dump("dbg_cs", cS[:, :], VE, VE.n)
    dump("dbg_ss", sS[:, :], VE, VE.n)
    dump("dbg_rt", r_tab[:, :], SE, SE.n)

    bt_f = [s_b[:, 0:1024], s_b[:, 1024:2048]]
    bt_s = [s_b[:, 2048:3072], s_b[:, 3072:4096]]
    m_sc = [0, 0]
    m_tp = [0, 0]
    m_st = [0, 0]
    prep_it = [0]

    def bc_prep(mat_list, tmo_t, tmo_ext):
        tmo_off = [0, 1024]
        for mat_src, mat_dst, scale in mat_list:
            for i in range(CH):
                pp = prep_it[0] % 2
                if m_sc[pp]:
                    SE.wait(nc.sync, m_sc[pp])
                src_ap = bass.AP(mat_src.ap().tensor, i * 128 * H,
                                 [[H, 128], [1, H]])
                BT[pp].inc(nc.sync.dma_start(bt_f[pp], src_ap))
                SX(BT[pp])
                if m_tp[pp]:
                    SX(TE, m_tp[pp])
                if scale == "g":
                    m_sc[pp] = S(lambda p=pp, ii=i: nc.scalar.activation(
                        bt_s[p], bt_f[p], AF.Identity,
                        scale=g_tab[:, ii:ii + 1]))
                else:
                    m_sc[pp] = S(lambda p=pp, sc2=scale: nc.scalar.activation(
                        bt_s[p], bt_f[p], AF.Identity, scale=float(sc2)))
                TX(SE, m_sc[pp])
                bk0 = pp_acquire()
                bk1 = pp_acquire()
                inst = None
                for j in range(8):
                    bb = bk0 if j < 4 else bk1
                    inst = nc.tensor.transpose(
                        PB[:, bb * 512 + (j % 4) * 128:
                           bb * 512 + (j % 4) * 128 + 128],
                        bt_s[pp][:, j * 128:j * 128 + 128], ident[:, :])
                m_tp[pp] = TE.inc(inst)
                SX(TE, m_tp[pp])
                if m_st[pp]:
                    SX(BST[pp])
                ev0 = S(lambda p=pp, b=bk0: nc.scalar.activation(
                    sbap(tmo_t, tmo_off[p], [[tmo_ext, 128], [1, 512]]),
                    PB[:, b * 512:(b + 1) * 512], AF.Identity))
                ev1 = S(lambda p=pp, b=bk1: nc.scalar.activation(
                    sbap(tmo_t, tmo_off[p] + 512, [[tmo_ext, 128], [1, 512]]),
                    PB[:, b * 512:(b + 1) * 512], AF.Identity))
                pp_release(bk0, "SE", ev0)
                pp_release(bk1, "SE", ev1)
                SE.wait(nc.scalar, ev1)
                dst_ap = bass.AP(mat_dst.ap().tensor, i * 128,
                                 [[H, 128], [128 * H, 8], [1, 128]])
                src2 = sbap(tmo_t, tmo_off[pp],
                            [[tmo_ext, 128], [128, 8], [1, 128]])
                m_st[pp] = BST[pp].inc(nc.scalar.dma_start(dst_ap, src2))
                prep_it[0] += 1

    TX(GP)
    bc_prep([(bre_d, sc_b[0], "g"), (bim_d, sc_b[1], "g")], xsre, XSEXT)

    def w_prep():
        for bch in BST:
            bch.wait(nc.gpsimd)
        wtmp = [h1[:, 0:4096], h1[:, 4096:8192]]
        seen = [False, False]
        for kc in range(CH):
            pp = kc % 2
            if seen[pp]:
                WST[pp].wait(nc.gpsimd)
            src_ap = bass.AP(w1_d.ap().tensor, kc * 128 * I,
                             [[I, 128], [1, I]])
            GDP[pp].inc(nc.gpsimd.dma_start(wtmp[pp], src_ap))
            GDP[pp].wait(nc.sync)
            dst_ap = bass.AP(sc_w1.ap().tensor, kc * 128 * I,
                             [[I, 128], [1, I]])
            WST[pp].inc(nc.sync.dma_start(dst_ap, wtmp[pp]))
            seen[pp] = True
        WST[0].wait(nc.sync)
        WST[1].wait(nc.sync)
        pump_hold.discard("w1")
        pump()
        wtmp2 = [h1[:, 0:1024], h1[:, 1024:2048]]
        for kc in range(CI):
            pp = kc % 2
            WST[pp].wait(nc.gpsimd)
            src_ap = bass.AP(w2_d.ap().tensor, kc * 128 * H,
                             [[H, 128], [1, H]])
            GDP[pp].inc(nc.gpsimd.dma_start(wtmp2[pp], src_ap))
            GDP[pp].wait(nc.sync)
            dst_ap = bass.AP(sc_w2.ap().tensor, kc * 128 * H,
                             [[H, 128], [1, H]])
            WST[pp].inc(nc.sync.dma_start(dst_ap, wtmp2[pp]))

    piece_marks = {}
    piece_use_marks = {}
    pend = []
    n_enq = [0]
    pump_hold = {"cre", "cim", "w1", "w2"}

    def piece_src(kind, q):
        if kind in ("bre", "bim", "cre", "cim"):
            t = {"bre": sc_b[0], "bim": sc_b[1],
                 "cre": sc_c[0], "cim": sc_c[1]}[kind]
            return bass.AP(t.ap().tensor, 0, [[H, 128], [128 * H, 8], [1, H]])
        if kind == "w1":
            return bass.AP(sc_w1.ap().tensor, q * 1024,
                           [[I, 128], [128 * I, 8], [1, 1024]])
        if kind == "w2":
            return bass.AP(sc_w2.ap().tensor, q * 8 * 128 * H,
                           [[H, 128], [128 * H, 8], [1, H]])
        raise ValueError(kind)

    def enq_piece(kind, q=0):
        j = n_enq[0]
        n_enq[0] += 1
        pend.append((j, kind, q))
        piece_use_marks[j] = None
        return j

    def pump():
        while pend:
            j, kind, q = pend[0]
            if kind in pump_hold:
                return
            if j >= 3:
                um = piece_use_marks[j - 3]
                if um is None:
                    return
                nc.sync.wait_ge(TE.h, um)
            piece_marks[j] = WS[j % 3].inc(
                nc.sync.dma_start(wslot[j % 3][:, :], piece_src(kind, q)))
            pend.pop(0)

    def pslot(j):
        return wslot[j % 3]

    def piece_wait(j):
        WS[j % 3].wait(nc.tensor, piece_marks[j])

    def piece_done(j, mark):
        piece_use_marks[j] = mark
        pump()

    for chx in BST:
        chx.wait(nc.sync)
    next_b = [enq_piece("bre"), enq_piece("bim")]
    pump()
    prev_cp = [None, None]
    prev_w = None


    for k in range(NSTEP + 1):
        has_cur = k < NSTEP
        has_prev = k >= 1
        cur_w = prev_w
        cur_cp = prev_cp
        xsl_k = xsl[k % 2]
        xnf_k = xnf[k % 2]
        xnf_p = xnf[(k - 1) % 2]


        if has_prev:
            jc0, jc1 = cur_cp
            pump()
            piece_wait(jc0)
            piece_wait(jc1)
            TX(VE, chain_done)
            last_cx = 0
            for m in range(CH):
                bk = pp_acquire()
                bank = PB[:, bk * 512:bk * 512 + 512]
                first = True
                inst = None
                for cpiece, xsbuf in [(jc0, xsre), (jc1, xsim)]:
                    for kk in range(CH):
                        inst = nc.tensor.matmul(
                            bank,
                            pslot(cpiece)[:, kk * 1024 + m * 128:
                                          kk * 1024 + (m + 1) * 128],
                            xsbuf[:, kk * NSUP:(kk + 1) * NSUP],
                            start=first,
                            stop=(cpiece == jc1 and kk == CH - 1))
                        first = False
                last_cx = TE.inc(inst)
                SX(TE, last_cx)
                sev = S(lambda mm=m, b=bank: nc.scalar.activation(
                    s_b[:, mm * NSUP:(mm + 1) * NSUP], b, AF.Identity))
                pp_release(bk, "SE", sev)
                if m == 5 and has_cur:
                    emit_ln_sqrt(k, 1)
                    emit_norm23(k)
            sevac_last = SE.n
            if DEBUG and k == 1:
                dump("dbg_s", s_b[:, :], VE, sevac_last)
            piece_done(jc0, last_cx)
            piece_done(jc1, last_cx)

        if has_cur:
            TX(SE, m_norm_of[k])
            if k == 0:
                TX(VE, m_identb)
            for sub in range(2):
                if sub == 1:
                    TX(SE, m_norm23_of[k])
                for c in range(CH):
                    bk = pp_acquire()
                    inst = None
                    for tbl in range(2):
                        tb = sub * 2 + tbl
                        dst = PB[:, bk * 512 + tbl * 128:
                                 bk * 512 + tbl * 128 + 128]
                        if sub == 0:
                            inst = nc.tensor.matmul(
                                dst,
                                xn_tok[:, (tb % 2) * H + c * 128:
                                       (tb % 2) * H + (c + 1) * 128],
                                identb[:, :], start=True, stop=True)
                        else:
                            inst = nc.tensor.transpose(
                                dst,
                                g_b[:, (tb % 2) * H + c * 128:
                                    (tb % 2) * H + (c + 1) * 128],
                                ident[:, :])
                    gm = TE.inc(inst)
                    SX(TE, gm)
                    if k >= 2:
                        SX(TE, w1_last[k % 2])
                        SX(VE, du_last[k % 2])
                    ev = S(lambda ss=sub, cc=c, b=bk:
                           nc.scalar.activation(
                               xnf_k[:, cc * NSUP + ss * NSUB:
                                     cc * NSUP + ss * NSUB + NSUB],
                               PB[:, b * 512:b * 512 + 256], AF.Identity))
                    pp_release(bk, "SE", ev)
            xnT_done_m = TE.n
            m_xn_ready = SE.n
            if DEBUG and k == 0:
                dump("dbg_xnf", xnf_k[:, :], SE, m_xn_ready)

        if has_cur:
            jb0, jb1 = next_b
            pump()
            piece_wait(jb0)
            piece_wait(jb1)
            TX(SE, m_xn_ready)
            if st_first_chain:
                SX(BST[0])
                SX(BST[1])
                st_first_chain = False
            bu_evac = [0] * CH
            rest_last = [0] * CH

            def emit_rest(m):
                cseg = cS[:, m * NSUP:(m + 1) * NSUP]
                sseg = sS[:, m * NSUP:(m + 1) * NSUP]
                vrr = vv[m][:, 0:NSUP]
                vii = vv[m][:, NSUP:2 * NSUP]
                yr = tmp[0][:, :]
                yi = tmp[1][:, :]
                rb, vrb = bass.broadcast_tensor_aps(r_tab[:, m:m + 1], vrr)
                V(lambda a=yr, d0=rb, d1=vrb, mm=m:
                  nc.vector.tensor_tensor_scan(
                      a, d0, d1, carry[:, 2 * mm:2 * mm + 1],
                      ALU.mult, ALU.add))
                rb2, vib = bass.broadcast_tensor_aps(r_tab[:, m:m + 1], vii)
                V(lambda a=yi, d0=rb2, d1=vib, mm=m:
                  nc.vector.tensor_tensor_scan(
                      a, d0, d1, carry[:, 2 * mm + 1:2 * mm + 2],
                      ALU.mult, ALU.add))
                V(lambda mm=m, a=yr: nc.vector.tensor_copy(
                    carry[:, 2 * mm:2 * mm + 1], a[:, NSUP - 1:NSUP]))
                V(lambda mm=m, a=yi: nc.vector.tensor_copy(
                    carry[:, 2 * mm + 1:2 * mm + 2], a[:, NSUP - 1:NSUP]))
                xr = xsre[:, m * NSUP:(m + 1) * NSUP]
                xi2 = xsim[:, m * NSUP:(m + 1) * NSUP]
                V(lambda a=xr, b=cseg, c2=yr:
                  nc.vector.tensor_tensor(a, b, c2, ALU.mult))
                V(lambda a=vrr, b=sseg, c2=yi:
                  nc.vector.tensor_tensor(a, b, c2, ALU.mult))
                V(lambda a=xr, b=vrr: nc.vector.tensor_tensor(
                    a, a, b, ALU.subtract))
                V(lambda a=vrr, b=sseg, c2=yr:
                  nc.vector.tensor_tensor(a, b, c2, ALU.mult))
                V(lambda a=vii, b=cseg, c2=yi:
                  nc.vector.tensor_tensor(a, b, c2, ALU.mult))
                rest_last[m] = V(lambda a=xi2, b=vrr, c2=vii:
                                 nc.vector.tensor_tensor(a, b, c2, ALU.add))

            bu_pp = [pp_acquire() for _ in range(4)]
            assert bu_pp == [0, 1, 2, 3], ("rotation misaligned", bu_pp)
            last_bu = 0
            for m in range(CH):
                slot = m % 4
                if slot < 2:
                    pt, base = PB, slot * 1024
                else:
                    pt, base = PA, (slot - 2) * 1024
                if m >= 4:
                    TX(SE, bu_evac[m - 4])
                elif slot >= 2 and k >= 1:
                    TX(SE, g_ev[1][2 * (slot - 2)])
                    TX(SE, g_ev[1][2 * (slot - 2) + 1])
                    TX(VE, outa_resid_prev[slot - 2])
                inst = None
                for comp, bj in [(0, jb0), (1, jb1)]:
                    reg = pt[:, base + comp * 512:base + comp * 512 + 512]
                    for kk in range(CH):
                        inst = nc.tensor.matmul(
                            reg,
                            pslot(bj)[:, kk * 1024 + m * 128:
                                      kk * 1024 + (m + 1) * 128],
                            xnf_k[:, kk * NSUP:(kk + 1) * NSUP],
                            start=(kk == 0), stop=(kk == CH - 1))
                last_bu = TE.inc(inst)

                SX(TE, last_bu)
                SX(VE, rest_last_prev[m])
                bu_evac[m] = S(lambda mm=m, p=pt, b=base:
                               nc.scalar.activation(
                                   vv[mm][:, :], p[:, b:b + 1024],
                                   AF.Identity))

                cseg = cS[:, m * NSUP:(m + 1) * NSUP]
                sseg = sS[:, m * NSUP:(m + 1) * NSUP]
                ur = vv[m][:, 0:NSUP]
                ui = vv[m][:, NSUP:2 * NSUP]
                xr = xsre[:, m * NSUP:(m + 1) * NSUP]
                xi2 = xsim[:, m * NSUP:(m + 1) * NSUP]
                VX(SE, bu_evac[m])
                V(lambda a=xr, b=ur, c2=cseg:
                  nc.vector.tensor_tensor(a, b, c2, ALU.mult))
                V(lambda a=xi2, b=ur, c2=sseg:
                  nc.vector.tensor_tensor(a, b, c2, ALU.mult))
                V(lambda a=ur, b=ui, c2=sseg:
                  nc.vector.tensor_tensor(a, b, c2, ALU.mult))
                V(lambda a=ui, c2=cseg:
                  nc.vector.tensor_tensor(a, a, c2, ALU.mult))
                V(lambda a=ur, b=xr: nc.vector.tensor_tensor(
                    a, b, a, ALU.add))
                V(lambda a=ui, b=xi2: nc.vector.tensor_tensor(
                    a, a, b, ALU.subtract))
                if m >= 2:
                    emit_rest(m - 2)
            if not has_prev:
                emit_rest(CH - 2)
                emit_rest(CH - 1)
                rest_last_prev = list(rest_last)
                if k + 1 < NSTEP:
                    if DEBUG and k == 0:
                        VX(DBG)
                    emit_rot(k + 1)
            else:
                emit_rest_deferred = emit_rest
            bu_evac_prev = bu_evac
            pp_release(bu_pp[0], "SE", bu_evac[4])
            pp_release(bu_pp[1], "SE", bu_evac[4])
            pp_release(bu_pp[2], "SE", bu_evac[5])
            pp_release(bu_pp[3], "SE", bu_evac[5])
            chain_done = VE.n
            if DEBUG and k == 0:
                dump("dbg_xsre", xsre[:, :], VE, chain_done)
                dump("dbg_xsim", xsim[:, :], VE, chain_done)
            piece_done(jb0, last_bu)
            piece_done(jb1, last_bu)
            prev_cp = [enq_piece("cre"), enq_piece("cim")]
            if k + 1 < NSTEP:
                next_b = [enq_piece("bre"), enq_piece("bim")]
            nw1 = [[0] * 4, [0] * 4]
            nw2 = [[0] * 4, [0] * 4]
            for sub in range(2):
                for q in range(4):
                    nw1[sub][q] = enq_piece("w1", q)
                for q in range(4):
                    nw2[sub][q] = enq_piece("w2", q)
            prev_w = (nw1, nw2)
            pump()

        if has_prev:
            new_gev = [[0] * 8, [0] * 8]
            resid_done = [0, 0]
            outa_resid = [0, 0]
            w1p, w2p = cur_w
            for sub in range(2):
                if sub == 0 and st_first_ffn:
                    SX(WST[0])
                    SX(WST[1])
                    st_first_ffn = False
                for q in range(4):
                    jw = w1p[sub][q]
                    pump()
                    piece_wait(jw)
                    inst = None
                    gm = 0
                    for ml in range(8):
                        bk = pp_acquire()
                        bank = PP[bk]
                        for kk in range(CH):
                            inst = nc.tensor.matmul(
                                bank[:, 0:256],
                                pslot(jw)[:, kk * 1024 + ml * 128:
                                          kk * 1024 + (ml + 1) * 128],
                                xnf_p[:, kk * NSUP + sub * NSUB:
                                      kk * NSUP + sub * NSUB + NSUB],
                                start=(kk == 0), stop=(kk == CH - 1))
                        gm = TE.inc(inst)
                        SX(TE, gm)
                        mi = q * 8 + ml
                        ev = S(lambda m2=mi, b=bank: nc.scalar.activation(
                            h1[:, m2 * NSUB:(m2 + 1) * NSUB],
                            b[:, 0:256], AF.Relu,
                            bias=b1_tab[:, m2:m2 + 1]))
                        pp_release(bk, "SE", ev)
                        h1_ev_last = ev
                    piece_done(jw, gm)
                w1_last[(k - 1) % 2] = TE.n
                if DEBUG and k == 1 and sub == 0:
                    dump("dbg_h1", h1[:, :], SE, h1_ev_last)
                if DEBUG and k == 1 and sub == 1:
                    SX(DBG)

                TX(SE, bu_evac_prev[6])
                TX(SE, bu_evac_prev[7])
                if sub == 1:
                    TX(VE, outa_resid[0])
                    TX(VE, outa_resid[1])
                TX(SE, h1_ev_last)
                SX(VE, chain_done)
                ppb = [pp_acquire() for _ in range(4)]
                w2bank = ([PA[:, j * 512:j * 512 + 256] for j in range(4)]
                          + [PB[:, b * 512:b * 512 + 256] for b in ppb])
                for q in range(4):
                    jw = w2p[sub][q]
                    pump()
                    piece_wait(jw)
                    inst = None
                    gm = 0
                    for m in range(CH):
                        if q == 0 and sub == 1:
                            TX(SE, new_gev[0][m])
                        for kl in range(CH):
                            inst = nc.tensor.matmul(
                                w2bank[m],
                                pslot(jw)[:, kl * 1024 + m * 128:
                                          kl * 1024 + (m + 1) * 128],
                                h1[:, (q * 8 + kl) * NSUB:
                                   (q * 8 + kl + 1) * NSUB],
                                start=(q == 0 and kl == 0),
                                stop=(q == 3 and kl == CH - 1),
                                skip_group_check=True)
                        if q == 3:
                            gm = TE.inc(inst)
                            SX(TE, gm)
                            new_gev[sub][m] = S(
                                lambda bnk=w2bank[m], mm=m:
                                nc.scalar.activation(
                                    g_b[:, mm * NSUB:(mm + 1) * NSUB], bnk,
                                    AF.Silu, bias=b2_tab[:, mm:mm + 1]))
                    if q < 3:
                        gm = TE.inc(inst)
                    piece_done(jw, gm)
                for jj, b in enumerate(ppb):
                    pp_release(b, "SE", new_gev[sub][4 + jj])

                if DEBUG and k == 1 and sub == 0:
                    dump("dbg_g", g_b[:, :], SE, new_gev[0][CH - 1])
                    VX(DBG)
                tmh = [0, 0]
                for hf in range(2):
                    VX(SE, new_gev[sub][hf * 4 + 3])
                    for mm in range(hf * 4, hf * 4 + 4):
                        V(lambda m2=mm:
                          nc.vector.scalar_tensor_tensor(
                              s_b[:, m2 * NSUP + sub * NSUB:
                                  m2 * NSUP + sub * NSUB + NSUB],
                              xnf_p[:, m2 * NSUP + sub * NSUB:
                                    m2 * NSUP + sub * NSUB + NSUB],
                              d_tab[:, m2:m2 + 1],
                              s_b[:, m2 * NSUP + sub * NSUB:
                                  m2 * NSUP + sub * NSUB + NSUB],
                              ALU.mult, ALU.add))
                    s_ap = sbap(s_b, sub * NSUB + hf * 4 * NSUP,
                                [[CH * NSUP, 128], [NSUP, 4], [1, NSUB]])
                    g_ap = sbap(g_b, hf * 4 * NSUB,
                                [[CH * NSUB, 128], [NSUB, 4], [1, NSUB]])
                    tmh[hf] = V(lambda sa=s_ap, ga=g_ap:
                                nc.vector.tensor_tensor(ga, sa, ga, ALU.mult))

                rm_last = 0
                for halfc in range(2):
                    for tbl in range(2):
                        tb = sub * 2 + tbl
                        if sub == 0:
                            bk2 = 2 * halfc + tbl
                            pdst = PA[:, bk2 * 512:bk2 * 512 + 512]
                            TX(SE, new_gev[0][bk2])
                        else:
                            bk = pp_acquire()
                            pdst = PB[:, bk * 512:bk * 512 + 512]
                        TX(VE, tmh[halfc])
                        inst = None
                        for c4 in range(4):
                            c = halfc * 4 + c4
                            inst = nc.tensor.transpose(
                                pdst[:, c4 * 128:c4 * 128 + 128],
                                g_b[:, c * NSUB + tbl * 128:
                                    c * NSUB + tbl * 128 + 128],
                                ident[:, :])
                        gm = TE.inc(inst)
                        VX(TE, gm)
                        xo = xsl[(k - 1) % 2][:, tb * H + halfc * 512:
                                              tb * H + halfc * 512 + 512]
                        rm = V(lambda a=xo, b2=pdst:
                               nc.vector.tensor_tensor(a, a, b2, ALU.add))
                        if sub == 0:
                            outa_resid[bk2 // 2] = rm
                        else:
                            pp_release(bk, "VE", rm)
                        rm_last = rm

                resid_done[sub] = rm_last
                VE.wait(nc.scalar, rm_last)
                srcx = sbap(xsl[(k - 1) % 2], sub * 2 * H,
                            [[XEXT, 128], [H, 2], [1, H]])
                out_dma[(k - 1, sub)] = SYO[sub].inc(
                    nc.scalar.dma_start(out_dst_ap(k - 1, sub), srcx))
                if k + 1 < NSTEP:
                    sl = (k + 1) % 2
                    SYO[sub].wait(nc.sync)
                    XS[sl][sub].inc(nc.sync.dma_start(
                        sbap(xsl[sl], sub * 2 * H,
                             [[XEXT, 128], [H, 2], [1, H]]),
                        x_src_half(k + 1, sub)))
                if sub == 0 and has_cur:
                    emit_rest_deferred(CH - 2)
                    emit_rest_deferred(CH - 1)
                    rest_last_prev = list(rest_last)
                    chain_done = VE.n
                if k + 1 < NSTEP and sub == 0:
                    emit_rot(k + 1)
                    emit_ln_v(k + 1, 0)
            g_ev = new_gev
            outa_resid_prev = outa_resid
            du_last[(k - 1) % 2] = VE.n
            if k + 1 < NSTEP:
                emit_ln_sqrt(k + 1, 0)
                emit_norm01(k + 1)

        if k == 0:
            bc_prep([(cre_d, sc_c[0], 1.0), (cim_d, sc_c[1], -1.0)],
                    h1, CI * NSUB)
            for chx in BST:
                chx.wait(nc.sync)
            pump_hold.discard("cre")
            pump_hold.discard("cim")
            pump()
            w_prep()
            for chx in WST:
                chx.wait(nc.sync)
            pump_hold.clear()
            pump()
        if k + 1 < NSTEP:
            if not has_prev:
                emit_ln_v(k + 1, 0)
                emit_ln_sqrt(k + 1, 0)
                emit_norm01(k + 1)
            emit_ln_v(k + 1, 1)

    for chx in ([SYP] + SYO + WS + XS[0] + XS[1] + BT + BST + GDP + WST
                + [DBG]):
        chx.wait(nc.sync)
    es.close()
    return nc


_CACHED = {}


def kernel(L0_attn_inputs, L0_attn_nu_log, L0_attn_theta_log, L0_attn_B_re,
           L0_attn_B_im, L0_attn_C_re, L0_attn_C_im, L0_attn_D,
           L0_attn_gamma_log, L0_ffn_W1, L0_ffn_b1, L0_ffn_W2, L0_ffn_b2):
    x = np.ascontiguousarray(np.asarray(L0_attn_inputs, dtype=np.float32))
    common = dict(
        nu_log=np.ascontiguousarray(L0_attn_nu_log, dtype=np.float32),
        theta_log=np.ascontiguousarray(L0_attn_theta_log, dtype=np.float32),
        B_re=np.ascontiguousarray(L0_attn_B_re, dtype=np.float32),
        B_im=np.ascontiguousarray(L0_attn_B_im, dtype=np.float32),
        C_re=np.ascontiguousarray(L0_attn_C_re, dtype=np.float32),
        C_im=np.ascontiguousarray(L0_attn_C_im, dtype=np.float32),
        Dv=np.ascontiguousarray(L0_attn_D, dtype=np.float32),
        gamma_log=np.ascontiguousarray(L0_attn_gamma_log, dtype=np.float32),
        W1=np.ascontiguousarray(L0_ffn_W1, dtype=np.float32),
        b1=np.ascontiguousarray(L0_ffn_b1, dtype=np.float32),
        W2=np.ascontiguousarray(L0_ffn_W2, dtype=np.float32),
        b2=np.ascontiguousarray(L0_ffn_b2, dtype=np.float32),
    )
    if "nc" not in _CACHED:
        _CACHED["nc"] = build_program()
    nc = _CACHED["nc"]
    in_maps = [dict(x=np.ascontiguousarray(x[c]), **common) for c in range(8)]
    res = run_bass_kernel_spmd(nc, in_maps, core_ids=list(range(8)))
    return np.stack([np.asarray(res.results[c]["out"]) for c in range(8)],
                    axis=0)


# version 31
# speedup vs baseline: 1.0643x; 1.0045x over previous
import numpy as np
from contextlib import ExitStack

import concourse.bass as bass
import concourse.mybir as mybir
from concourse.bass_utils import run_bass_kernel_spmd

F32 = mybir.dt.float32
BF16 = mybir.dt.bfloat16
AF = mybir.ActivationFunctionType
ALU = mybir.AluOpType

H = 1024
I = 4096
T = 8192
NSUB = 256
NSUP = 512
NSTEP = T // NSUP
CH = H // 128
CI = I // 128
EPS = 1e-5
PI = float(np.pi)
TWO_PI = float(2 * np.pi)
RND_M = 12582912.0
DEBUG = False


class Ch:
    def __init__(self, nc, es, name, step=1):
        self.h = es.enter_context(nc.semaphore(name))
        self.n = 0
        self.step = step

    def inc(self, inst):
        inst.then_inc(self.h, self.step)
        self.n += self.step
        return self.n

    def wait(self, eng, mark=None):
        v = self.n if mark is None else mark
        if v and v > 0:
            eng.wait_ge(self.h, v)


def build_program():
    nc = bass.Bass("TRN2", target_bir_lowering=False, debug=False)

    x_d = nc.dram_tensor("x", [T, H], F32, kind="ExternalInput")
    nu_d = nc.dram_tensor("nu_log", [H], F32, kind="ExternalInput")
    th_d = nc.dram_tensor("theta_log", [H], F32, kind="ExternalInput")
    bre_d = nc.dram_tensor("B_re", [H, H], F32, kind="ExternalInput")
    bim_d = nc.dram_tensor("B_im", [H, H], F32, kind="ExternalInput")
    cre_d = nc.dram_tensor("C_re", [H, H], F32, kind="ExternalInput")
    cim_d = nc.dram_tensor("C_im", [H, H], F32, kind="ExternalInput")
    dd_d = nc.dram_tensor("Dv", [H], F32, kind="ExternalInput")
    gl_d = nc.dram_tensor("gamma_log", [H], F32, kind="ExternalInput")
    w1_d = nc.dram_tensor("W1", [H, I], F32, kind="ExternalInput")
    b1_d = nc.dram_tensor("b1", [I], F32, kind="ExternalInput")
    w2_d = nc.dram_tensor("W2", [I, H], F32, kind="ExternalInput")
    b2_d = nc.dram_tensor("b2", [H], F32, kind="ExternalInput")
    out_d = nc.dram_tensor("out", [T, H], F32, kind="ExternalOutput")

    sc_b = [nc.dram_tensor(f"sc_b{c}", [H, H], BF16, kind="Internal")
            for c in range(2)]
    sc_c = [nc.dram_tensor(f"sc_c{c}", [H, H], BF16, kind="Internal")
            for c in range(2)]
    sc_w1 = nc.dram_tensor("sc_w1", [H, I], BF16, kind="Internal")
    sc_w2 = nc.dram_tensor("sc_w2", [I, H], BF16, kind="Internal")
    dbg = {}
    if DEBUG:
        for nm, shp, dt in [("dbg_xn", [128, 2048], BF16),
                            ("dbg_xnf", [128, 4096], BF16),
                            ("dbg_xsre", [128, 4096], BF16),
                            ("dbg_xsim", [128, 4096], BF16),
                            ("dbg_s", [128, 4096], F32),
                            ("dbg_h1", [128, 8192], BF16),
                            ("dbg_g", [128, 2048], F32),
                            ("dbg_cs", [128, 4096], F32),
                            ("dbg_ss", [128, 4096], F32),
                            ("dbg_rt", [128, 8], F32)]:
            dbg[nm] = nc.dram_tensor(nm, shp, dt, kind="ExternalOutput")

    es = ExitStack()

    def sb(name, shape, dtype):
        return es.enter_context(nc.sbuf_tensor(name, shape, dtype))

    def psum(name, shape):
        return es.enter_context(nc.psum_tensor(name, shape, F32))

    wslot = [sb(f"wslot{i}", [128, 8 * 1024], BF16) for i in range(3)]
    cS = sb("cS", [128, CH * NSUP], F32)
    sS = sb("sS", [128, CH * NSUP], F32)
    vv = [sb(f"vv{i}", [128, 2 * NSUP], BF16) for i in range(CH)]
    tmp0 = sb("tmp0", [128, NSUP], F32)
    xsre = sb("xsre", [128, CH * NSUP], BF16)
    xsim = sb("xsim", [128, CH * NSUP], BF16)
    s_b = sb("s_b", [128, CH * NSUP], F32)
    g_b = sb("g_b", [128, CH * NSUB], F32)
    tmp = [tmp0, g_b[:, 0:NSUP]]
    h1 = sb("h1", [128, CI * NSUB], BF16)
    xsl = [sb(f"xsl{i}", [128, 4 * H], F32) for i in range(2)]
    xn_tok = sb("xn_tok", [128, 2 * H], BF16)
    xnf = [sb(f"xnf{i}", [128, CH * NSUP], BF16) for i in range(2)]
    carry = sb("carry", [128, 2 * CH + 16], F32)
    r_tab = sb("r_tab", [128, CH], F32)
    d_tab = sb("d_tab", [128, CH], F32)
    g_tab = sb("g_tab", [128, CH], F32)
    b1_tab = sb("b1_tab", [128, CI], F32)
    b2_tab = sb("b2_tab", [128, CH], F32)
    cD = sb("cD", [128, CH], F32)
    sD = sb("sD", [128, CH], F32)
    ident = sb("ident", [128, 128], F32)
    identb = sb("identb", [128, 128], BF16)

    PA = psum("PA", [128, 2048])
    PB = psum("PB", [128, 2048])
    PP = [PB[:, b * 512:(b + 1) * 512] for b in range(4)]

    TE = Ch(nc, es, "TE")
    VE = Ch(nc, es, "VE")
    SE = Ch(nc, es, "SE")
    GP = Ch(nc, es, "GP")
    SYP = Ch(nc, es, "SYP", step=16)
    BT = [Ch(nc, es, f"BT{i}", step=16) for i in range(4)]
    BST = [Ch(nc, es, f"BST{i}", step=16) for i in range(4)]
    GDP = [Ch(nc, es, f"GDP{i}", step=16) for i in range(2)]
    WST = [Ch(nc, es, f"WST{i}", step=16) for i in range(2)]
    WS = [Ch(nc, es, f"WS{i}", step=16) for i in range(3)]
    XS = [[Ch(nc, es, f"XS{i}_{h}", step=16) for h in range(2)]
          for i in range(2)]
    SYO = [Ch(nc, es, f"SYO{i}", step=16) for i in range(2)]
    DBG = Ch(nc, es, "DBG", step=16)

    def dump(nm, src_ap, ch, mark):
        if not DEBUG:
            return
        DBG.wait(nc.sync)
        ch.wait(nc.sync, mark)
        DBG.inc(nc.sync.dma_start(dbg[nm].ap(), src_ap))

    def V(emit):
        VE.wait(nc.vector)
        return VE.inc(emit())

    def S(emit):
        SE.wait(nc.scalar)
        return SE.inc(emit())

    def VX(ch, mark=None):
        ch.wait(nc.vector, mark)

    def SX(ch, mark=None):
        ch.wait(nc.scalar, mark)

    def TX(ch, mark=None):
        ch.wait(nc.tensor, mark)

    def sbap(t, off, dims):
        return bass.AP(t.tensor if isinstance(t, bass.AP) else t, off, dims)

    def x_src_ap(k):
        return bass.AP(x_d.ap().tensor, k * NSUP * H,
                       [[H, 128], [128 * H, 4], [1, H]])

    def x_src_half(k, h):
        return bass.AP(x_d.ap().tensor, (k * NSUP + h * 256) * H,
                       [[H, 128], [128 * H, 2], [1, H]])

    def out_dst_ap(k, sub):
        return bass.AP(out_d.ap().tensor, (k * NSUP + sub * NSUB) * H,
                       [[H, 128], [128 * H, 2], [1, H]])

    XEXT = 4 * H
    XSEXT = CH * NSUP

    pp_state = {"i": 0, "hist": [("SE", 0)] * 4}

    def pp_acquire():
        b = pp_state["i"] % 4
        kind, mark = pp_state["hist"][b]
        if kind == "SE":
            TX(SE, mark)
        else:
            TX(VE, mark)
        pp_state["i"] += 1
        return b

    def pp_release(b, kind, mark):
        pp_state["hist"][b] = (kind, mark)

    m_norm_of = {}

    def emit_rot(kk):
        if kk == 0:
            return
        t_a = tmp0[:, 0:NSUB]
        t_b = tmp0[:, NSUB:2 * NSUB]
        for m in range(CH):
            for hf in range(2):
                off = m * NSUP + hf * NSUB
                cseg = cS[:, off:off + NSUB]
                sseg = sS[:, off:off + NSUB]
                V(lambda a=t_a, b=sseg, s2=sD[:, m:m + 1]:
                  nc.vector.tensor_scalar(a, b, s2, None, ALU.mult))
                V(lambda a=t_b, b=cseg, s2=sD[:, m:m + 1]:
                  nc.vector.tensor_scalar(a, b, s2, None, ALU.mult))
                V(lambda a=cseg, s2=cD[:, m:m + 1], t=t_a:
                  nc.vector.scalar_tensor_tensor(
                      a, a, s2, t, ALU.mult, ALU.subtract))
                V(lambda a=sseg, s2=cD[:, m:m + 1], t=t_b:
                  nc.vector.scalar_tensor_tensor(
                      a, a, s2, t, ALU.mult, ALU.add))

    m_veps_of = {}

    def emit_ln_v(kk, h):
        xslk = xsl[kk % 2]
        XS[kk % 2][h].wait(nc.vector)
        for tb in (2 * h, 2 * h + 1):
            seg = xslk[:, tb * H:(tb + 1) * H]
            V(lambda s1=seg: nc.vector.bn_stats(tmp[0][:, 0:6],
                                                s1[:, 0:512]))
            V(lambda s1=seg: nc.vector.bn_stats(tmp[0][:, 6:12],
                                                s1[:, 512:1024]))
            V(lambda t=tb: nc.vector.bn_aggr(
                tmp[0][:, 16 + 2 * t:18 + 2 * t], tmp[0][:, 0:12]))
        vap = sbap(tmp[0], 17 + 4 * h, [[NSUP, 128], [2, 2]])
        veps = carry[:, 16 + 2 * h:18 + 2 * h]
        V(lambda a=veps, b=vap: nc.vector.tensor_scalar(
            a, b, EPS, None, ALU.add))
        m_veps_of[(kk, h)] = VE.n

    m_lnfin = {}
    m_norm23_of = {}

    def emit_ln_sqrt(kk, h):
        map_ = sbap(tmp[0], 16 + 4 * h, [[NSUP, 128], [2, 2]])
        veps = carry[:, 16 + 2 * h:18 + 2 * h]
        inv2 = carry[:, 20 + 2 * h:22 + 2 * h]
        nm2 = carry[:, 24 + 2 * h:26 + 2 * h]
        SX(VE, m_veps_of[(kk, h)])
        m_sq = S(lambda a=veps: nc.scalar.activation(a, a, AF.Sqrt))
        VX(SE, m_sq)
        V(lambda a=inv2, b=veps: nc.vector.reciprocal(a, b))
        V(lambda a=nm2, b=map_, c2=inv2: nc.vector.tensor_tensor(
            a, b, c2, ALU.mult))
        m_lnfin[(kk, h)] = V(lambda a=nm2: nc.vector.tensor_scalar(
            a, a, -1.0, None, ALU.mult))

    def emit_norm01(kk):
        xslk = xsl[kk % 2]
        SX(VE, m_lnfin[(kk, 0)])
        SX(TE, xnT_done_m)
        m_norm = 0
        for tb in range(2):
            m_norm = S(lambda t=tb: nc.scalar.activation(
                xn_tok[:, t * H:(t + 1) * H],
                xslk[:, t * H:(t + 1) * H],
                AF.Identity, scale=carry[:, 20 + t:21 + t],
                bias=carry[:, 24 + t:25 + t]))
        m_norm_of[kk] = m_norm

    def emit_norm23(kk):
        xslk = xsl[kk % 2]
        SX(VE, m_lnfin[(kk, 1)])
        m_norm = 0
        for t in (2, 3):
            m_norm = S(lambda t2=t: nc.scalar.activation(
                g_b[:, (t2 % 2) * H:(t2 % 2) * H + H],
                xslk[:, t2 * H:(t2 + 1) * H],
                AF.Identity, scale=carry[:, 20 + t2:21 + t2],
                bias=carry[:, 24 + t2:25 + t2]))
        m_norm23_of[kk] = m_norm

    chain_done = 0
    sevac_last = 0
    du_last = [0, 0]
    g_ev = [[0] * 8, [0] * 8]
    w1_last = [0, 0]
    xnT_done_m = 0
    out_dma = {}
    st_first_chain = True
    outa_resid_prev = [0, 0]
    rest_last_prev = [0] * CH
    bu_evac_prev = [0] * CH
    st_first_ffn = True

    for src, dst, nch in [(nu_d, tmp0[:, 64:72], CH),
                          (th_d, tmp0[:, 72:80], CH),
                          (gl_d, g_tab[:, :], CH), (dd_d, d_tab[:, :], CH),
                          (b1_d, b1_tab[:, :], CI), (b2_d, b2_tab[:, :], CH)]:
        ap = bass.AP(src.ap().tensor, 0, [[1, 128], [128, nch]])
        SYP.inc(nc.sync.dma_start(dst, ap,
                                  allow_slow_non_contiguous=True))
    m_x = {}
    for sl2 in range(2):
        for h2 in range(2):
            XS[sl2][h2].inc(nc.sync.dma_start(
                sbap(xsl[sl2], h2 * 2 * H, [[XEXT, 128], [H, 2], [1, H]]),
                x_src_half(sl2, h2)))
    m_x[0] = None
    m_x[1] = None

    GP.inc(nc.gpsimd.memset(tmp0[:, 384:512], 1.0))
    GP.wait(nc.gpsimd)
    GP.inc(nc.gpsimd.affine_select(
        ident[:, :], tmp0[:, 384:512], [[1, 128]], ALU.is_equal, 0.0,
        base=0, channel_multiplier=-1))

    SX(SYP)
    S(lambda: nc.scalar.activation(tmp0[:, 96:104], tmp0[:, 64:72], AF.Exp))
    S(lambda: nc.scalar.activation(r_tab[:, :], tmp0[:, 96:104], AF.Exp,
                                   scale=-1.0))
    S(lambda: nc.scalar.activation(tmp0[:, 72:80], tmp0[:, 72:80], AF.Exp))
    S(lambda: nc.scalar.activation(g_tab[:, :], g_tab[:, :], AF.Exp))
    m_gam = SE.n

    VX(SE, m_gam)
    V(lambda: nc.vector.tensor_scalar(tmp0[:, 104:112], tmp0[:, 72:80],
                                      1.0 / TWO_PI, None, ALU.mult))
    V(lambda: nc.vector.tensor_scalar(tmp0[:, 104:112], tmp0[:, 104:112],
                                      RND_M, -RND_M, ALU.add, ALU.add))
    V(lambda: nc.vector.scalar_tensor_tensor(
        tmp0[:, 112:120], tmp0[:, 104:112], -TWO_PI, tmp0[:, 72:80], ALU.mult, ALU.add))
    m_thw = VE.n
    SX(VE, m_thw)
    S(lambda: nc.scalar.activation(tmp0[:, 88:96], tmp0[:, 112:120], AF.Sin))
    V(lambda: nc.vector.tensor_scalar(tmp0[:, 120:128], tmp0[:, 112:120],
                                      PI / 2, None, ALU.add))
    V(lambda: nc.vector.tensor_scalar(tmp0[:, 104:112], tmp0[:, 120:128],
                                      1.0 / TWO_PI, None, ALU.mult))
    V(lambda: nc.vector.tensor_scalar(tmp0[:, 104:112], tmp0[:, 104:112],
                                      RND_M, -RND_M, ALU.add, ALU.add))
    V(lambda: nc.vector.scalar_tensor_tensor(
        tmp0[:, 120:128], tmp0[:, 104:112], -TWO_PI, tmp0[:, 120:128], ALU.mult, ALU.add))
    m_cw = VE.n
    SX(VE, m_cw)
    S(lambda: nc.scalar.activation(tmp0[:, 80:88], tmp0[:, 120:128], AF.Sin))
    m_cs1 = SE.n

    emit_ln_v(0, 0)
    emit_ln_v(0, 1)
    emit_ln_sqrt(0, 0)
    emit_norm01(0)
    emit_ln_sqrt(0, 1)
    emit_norm23(0)
    CEXT = CH * NSUP
    V(lambda: nc.vector.memset(sbap(cS, 0, [[CEXT, 128], [NSUP, CH]]), 1.0))
    V(lambda: nc.vector.memset(sbap(sS, 0, [[CEXT, 128], [NSUP, CH]]), 0.0))
    VX(SE, m_cs1)
    VX(GP)
    V(lambda: nc.vector.tensor_copy(identb[:, :], ident[:, :]))
    m_identb = VE.n
    L = 1
    while L < NSUP:
        for m in range(CH):
            base = m * NSUP
            cseg = cS[:, base:base + L]
            sseg = sS[:, base:base + L]
            cdst = cS[:, base + L:base + 2 * L]
            sdst = sS[:, base + L:base + 2 * L]
            cl = tmp0[:, 80 + m:81 + m]
            sl = tmp0[:, 88 + m:89 + m]
            V(lambda a=tmp0[:, 256:256 + L], b=sseg, s2=sl:
              nc.vector.tensor_scalar(a, b, s2, None, ALU.mult))
            V(lambda a=cdst, b=cseg, s2=cl, c2=tmp0[:, 256:256 + L]:
              nc.vector.scalar_tensor_tensor(a, b, s2, c2,
                                             ALU.mult, ALU.subtract))
            V(lambda a=tmp0[:, 256:256 + L], b=cseg, s2=sl:
              nc.vector.tensor_scalar(a, b, s2, None, ALU.mult))
            V(lambda a=sdst, b=sseg, s2=cl, c2=tmp0[:, 256:256 + L]:
              nc.vector.scalar_tensor_tensor(a, b, s2, c2,
                                             ALU.mult, ALU.add))
        V(lambda: nc.vector.tensor_tensor(tmp0[:, 96:104], tmp0[:, 80:88], tmp0[:, 80:88],
                                          ALU.mult))
        V(lambda: nc.vector.tensor_tensor(tmp0[:, 104:112], tmp0[:, 88:96], tmp0[:, 88:96],
                                          ALU.mult))
        V(lambda: nc.vector.tensor_tensor(tmp0[:, 112:120], tmp0[:, 80:88], tmp0[:, 88:96],
                                          ALU.mult))
        V(lambda: nc.vector.tensor_tensor(tmp0[:, 80:88], tmp0[:, 96:104], tmp0[:, 104:112],
                                          ALU.subtract))
        V(lambda: nc.vector.tensor_scalar(tmp0[:, 88:96], tmp0[:, 112:120], 2.0, None,
                                          ALU.mult))
        L *= 2
    V(lambda: nc.vector.tensor_copy(cD[:, :], tmp0[:, 80:88]))
    V(lambda: nc.vector.tensor_copy(sD[:, :], tmp0[:, 88:96]))
    V(lambda: nc.vector.memset(carry[:, 0:2 * CH], 0.0))
    dump("dbg_cs", cS[:, :], VE, VE.n)
    dump("dbg_ss", sS[:, :], VE, VE.n)
    dump("dbg_rt", r_tab[:, :], SE, SE.n)

    bt_f = [s_b[:, 0:1024], s_b[:, 1024:2048]]
    bt_s = [s_b[:, 2048:3072], s_b[:, 3072:4096]]
    m_sc = [0, 0]
    m_tp = [0, 0]
    m_st = [0, 0]
    prep_it = [0]

    def bc_prep(mat_list, tmo_t, tmo_ext):
        tmo_off = [0, 1024]
        for mat_src, mat_dst, scale in mat_list:
            for i in range(CH):
                pp = prep_it[0] % 2
                if m_sc[pp]:
                    SE.wait(nc.sync, m_sc[pp])
                src_ap = bass.AP(mat_src.ap().tensor, i * 128 * H,
                                 [[H, 128], [1, H]])
                BT[pp].inc(nc.sync.dma_start(bt_f[pp], src_ap))
                SX(BT[pp])
                if m_tp[pp]:
                    SX(TE, m_tp[pp])
                if scale == "g":
                    m_sc[pp] = S(lambda p=pp, ii=i: nc.scalar.activation(
                        bt_s[p], bt_f[p], AF.Identity,
                        scale=g_tab[:, ii:ii + 1]))
                else:
                    m_sc[pp] = S(lambda p=pp, sc2=scale: nc.scalar.activation(
                        bt_s[p], bt_f[p], AF.Identity, scale=float(sc2)))
                TX(SE, m_sc[pp])
                bk0 = pp_acquire()
                bk1 = pp_acquire()
                inst = None
                for j in range(8):
                    bb = bk0 if j < 4 else bk1
                    inst = nc.tensor.transpose(
                        PB[:, bb * 512 + (j % 4) * 128:
                           bb * 512 + (j % 4) * 128 + 128],
                        bt_s[pp][:, j * 128:j * 128 + 128], ident[:, :])
                m_tp[pp] = TE.inc(inst)
                SX(TE, m_tp[pp])
                if m_st[pp]:
                    SX(BST[pp])
                ev0 = S(lambda p=pp, b=bk0: nc.scalar.activation(
                    sbap(tmo_t, tmo_off[p], [[tmo_ext, 128], [1, 512]]),
                    PB[:, b * 512:(b + 1) * 512], AF.Identity))
                ev1 = S(lambda p=pp, b=bk1: nc.scalar.activation(
                    sbap(tmo_t, tmo_off[p] + 512, [[tmo_ext, 128], [1, 512]]),
                    PB[:, b * 512:(b + 1) * 512], AF.Identity))
                pp_release(bk0, "SE", ev0)
                pp_release(bk1, "SE", ev1)
                SE.wait(nc.scalar, ev1)
                dst_ap = bass.AP(mat_dst.ap().tensor, i * 128,
                                 [[H, 128], [128 * H, 8], [1, 128]])
                src2 = sbap(tmo_t, tmo_off[pp],
                            [[tmo_ext, 128], [128, 8], [1, 128]])
                m_st[pp] = BST[pp].inc(nc.scalar.dma_start(dst_ap, src2))
                prep_it[0] += 1

    TX(GP)
    bc_prep([(bre_d, sc_b[0], "g"), (bim_d, sc_b[1], "g")], xsre, XSEXT)

    def w1_cast():
        for bch in BST:
            bch.wait(nc.gpsimd)
        wtmp = [h1[:, 0:4096], h1[:, 4096:8192]]
        seen = [False, False]
        for kc in range(CH):
            pp = kc % 2
            if seen[pp]:
                WST[pp].wait(nc.gpsimd)
            src_ap = bass.AP(w1_d.ap().tensor, kc * 128 * I,
                             [[I, 128], [1, I]])
            GDP[pp].inc(nc.gpsimd.dma_start(wtmp[pp], src_ap))
            GDP[pp].wait(nc.sync)
            dst_ap = bass.AP(sc_w1.ap().tensor, kc * 128 * I,
                             [[I, 128], [1, I]])
            WST[pp].inc(nc.sync.dma_start(dst_ap, wtmp[pp]))
            seen[pp] = True

    def w2_cast():
        for bch in BST:
            bch.wait(nc.gpsimd)
        wtmp2 = [h1[:, 0:1024], h1[:, 1024:2048]]
        for kc in range(CI):
            pp = kc % 2
            WST[pp].wait(nc.gpsimd)
            src_ap = bass.AP(w2_d.ap().tensor, kc * 128 * H,
                             [[H, 128], [1, H]])
            GDP[pp].inc(nc.gpsimd.dma_start(wtmp2[pp], src_ap))
            GDP[pp].wait(nc.sync)
            dst_ap = bass.AP(sc_w2.ap().tensor, kc * 128 * H,
                             [[H, 128], [1, H]])
            WST[pp].inc(nc.sync.dma_start(dst_ap, wtmp2[pp]))

    piece_marks = {}
    piece_use_marks = {}
    pend = []
    n_enq = [0]
    pump_hold = {"cre", "cim", "w1", "w2"}

    def piece_src(kind, q):
        if kind in ("bre", "bim", "cre", "cim"):
            t = {"bre": sc_b[0], "bim": sc_b[1],
                 "cre": sc_c[0], "cim": sc_c[1]}[kind]
            return bass.AP(t.ap().tensor, 0, [[H, 128], [128 * H, 8], [1, H]])
        if kind == "w1":
            return bass.AP(sc_w1.ap().tensor, q * 1024,
                           [[I, 128], [128 * I, 8], [1, 1024]])
        if kind == "w2":
            return bass.AP(sc_w2.ap().tensor, q * 8 * 128 * H,
                           [[H, 128], [128 * H, 8], [1, H]])
        raise ValueError(kind)

    def enq_piece(kind, q=0):
        j = n_enq[0]
        n_enq[0] += 1
        pend.append((j, kind, q))
        piece_use_marks[j] = None
        return j

    def pump():
        while pend:
            j, kind, q = pend[0]
            if kind in pump_hold:
                return
            if j >= 3:
                um = piece_use_marks[j - 3]
                if um is None:
                    return
                nc.sync.wait_ge(TE.h, um)
            piece_marks[j] = WS[j % 3].inc(
                nc.sync.dma_start(wslot[j % 3][:, :], piece_src(kind, q)))
            pend.pop(0)

    def pslot(j):
        return wslot[j % 3]

    def piece_wait(j):
        WS[j % 3].wait(nc.tensor, piece_marks[j])

    def piece_done(j, mark):
        piece_use_marks[j] = mark
        pump()

    for chx in BST:
        chx.wait(nc.sync)
    next_b = [enq_piece("bre"), enq_piece("bim")]
    pump()
    w1_cast()
    prev_cp = [None, None]
    prev_w = None


    for k in range(NSTEP + 1):
        has_cur = k < NSTEP
        has_prev = k >= 1
        cur_w = prev_w
        cur_cp = prev_cp
        xsl_k = xsl[k % 2]
        xnf_k = xnf[k % 2]
        xnf_p = xnf[(k - 1) % 2]


        if has_prev:
            jc0, jc1 = cur_cp
            pump()
            piece_wait(jc0)
            piece_wait(jc1)
            TX(VE, chain_done)
            last_cx = 0
            for m in range(CH):
                bk = pp_acquire()
                bank = PB[:, bk * 512:bk * 512 + 512]
                first = True
                inst = None
                for cpiece, xsbuf in [(jc0, xsre), (jc1, xsim)]:
                    for kk in range(CH):
                        inst = nc.tensor.matmul(
                            bank,
                            pslot(cpiece)[:, kk * 1024 + m * 128:
                                          kk * 1024 + (m + 1) * 128],
                            xsbuf[:, kk * NSUP:(kk + 1) * NSUP],
                            start=first,
                            stop=(cpiece == jc1 and kk == CH - 1))
                        first = False
                last_cx = TE.inc(inst)
                SX(TE, last_cx)
                sev = S(lambda mm=m, b=bank: nc.scalar.activation(
                    s_b[:, mm * NSUP:(mm + 1) * NSUP], b, AF.Identity))
                pp_release(bk, "SE", sev)
                if m == 5 and has_cur:
                    emit_ln_sqrt(k, 1)
                    emit_norm23(k)
            sevac_last = SE.n
            if DEBUG and k == 1:
                dump("dbg_s", s_b[:, :], VE, sevac_last)
            piece_done(jc0, last_cx)
            piece_done(jc1, last_cx)

        if has_cur:
            TX(SE, m_norm_of[k])
            if k == 0:
                TX(VE, m_identb)
            for sub in range(2):
                if sub == 1:
                    TX(SE, m_norm23_of[k])
                for c in range(CH):
                    bk = pp_acquire()
                    inst = None
                    for tbl in range(2):
                        tb = sub * 2 + tbl
                        dst = PB[:, bk * 512 + tbl * 128:
                                 bk * 512 + tbl * 128 + 128]
                        if sub == 0:
                            inst = nc.tensor.matmul(
                                dst,
                                xn_tok[:, (tb % 2) * H + c * 128:
                                       (tb % 2) * H + (c + 1) * 128],
                                identb[:, :], start=True, stop=True)
                        else:
                            inst = nc.tensor.transpose(
                                dst,
                                g_b[:, (tb % 2) * H + c * 128:
                                    (tb % 2) * H + (c + 1) * 128],
                                ident[:, :])
                    gm = TE.inc(inst)
                    SX(TE, gm)
                    if k >= 2:
                        SX(TE, w1_last[k % 2])
                        SX(VE, du_last[k % 2])
                    ev = S(lambda ss=sub, cc=c, b=bk:
                           nc.scalar.activation(
                               xnf_k[:, cc * NSUP + ss * NSUB:
                                     cc * NSUP + ss * NSUB + NSUB],
                               PB[:, b * 512:b * 512 + 256], AF.Identity))
                    pp_release(bk, "SE", ev)
            xnT_done_m = TE.n
            m_xn_ready = SE.n
            if DEBUG and k == 0:
                dump("dbg_xnf", xnf_k[:, :], SE, m_xn_ready)

        if has_cur:
            jb0, jb1 = next_b
            pump()
            piece_wait(jb0)
            piece_wait(jb1)
            TX(SE, m_xn_ready)
            if st_first_chain:
                SX(BST[0])
                SX(BST[1])
                st_first_chain = False
            bu_evac = [0] * CH
            rest_last = [0] * CH

            def emit_rest(m):
                cseg = cS[:, m * NSUP:(m + 1) * NSUP]
                sseg = sS[:, m * NSUP:(m + 1) * NSUP]
                vrr = vv[m][:, 0:NSUP]
                vii = vv[m][:, NSUP:2 * NSUP]
                yr = tmp[0][:, :]
                yi = tmp[1][:, :]
                rb, vrb = bass.broadcast_tensor_aps(r_tab[:, m:m + 1], vrr)
                V(lambda a=yr, d0=rb, d1=vrb, mm=m:
                  nc.vector.tensor_tensor_scan(
                      a, d0, d1, carry[:, 2 * mm:2 * mm + 1],
                      ALU.mult, ALU.add))
                rb2, vib = bass.broadcast_tensor_aps(r_tab[:, m:m + 1], vii)
                V(lambda a=yi, d0=rb2, d1=vib, mm=m:
                  nc.vector.tensor_tensor_scan(
                      a, d0, d1, carry[:, 2 * mm + 1:2 * mm + 2],
                      ALU.mult, ALU.add))
                V(lambda mm=m, a=yr: nc.vector.tensor_copy(
                    carry[:, 2 * mm:2 * mm + 1], a[:, NSUP - 1:NSUP]))
                V(lambda mm=m, a=yi: nc.vector.tensor_copy(
                    carry[:, 2 * mm + 1:2 * mm + 2], a[:, NSUP - 1:NSUP]))
                xr = xsre[:, m * NSUP:(m + 1) * NSUP]
                xi2 = xsim[:, m * NSUP:(m + 1) * NSUP]
                V(lambda a=xr, b=cseg, c2=yr:
                  nc.vector.tensor_tensor(a, b, c2, ALU.mult))
                V(lambda a=vrr, b=sseg, c2=yi:
                  nc.vector.tensor_tensor(a, b, c2, ALU.mult))
                V(lambda a=xr, b=vrr: nc.vector.tensor_tensor(
                    a, a, b, ALU.subtract))
                V(lambda a=vrr, b=sseg, c2=yr:
                  nc.vector.tensor_tensor(a, b, c2, ALU.mult))
                V(lambda a=vii, b=cseg, c2=yi:
                  nc.vector.tensor_tensor(a, b, c2, ALU.mult))
                rest_last[m] = V(lambda a=xi2, b=vrr, c2=vii:
                                 nc.vector.tensor_tensor(a, b, c2, ALU.add))

            bu_pp = [pp_acquire() for _ in range(4)]
            assert bu_pp == [0, 1, 2, 3], ("rotation misaligned", bu_pp)
            last_bu = 0
            for m in range(CH):
                slot = m % 4
                if slot < 2:
                    pt, base = PB, slot * 1024
                else:
                    pt, base = PA, (slot - 2) * 1024
                if m >= 4:
                    TX(SE, bu_evac[m - 4])
                elif slot >= 2 and k >= 1:
                    TX(SE, g_ev[1][2 * (slot - 2)])
                    TX(SE, g_ev[1][2 * (slot - 2) + 1])
                    TX(VE, outa_resid_prev[slot - 2])
                inst = None
                for comp, bj in [(0, jb0), (1, jb1)]:
                    reg = pt[:, base + comp * 512:base + comp * 512 + 512]
                    for kk in range(CH):
                        inst = nc.tensor.matmul(
                            reg,
                            pslot(bj)[:, kk * 1024 + m * 128:
                                      kk * 1024 + (m + 1) * 128],
                            xnf_k[:, kk * NSUP:(kk + 1) * NSUP],
                            start=(kk == 0), stop=(kk == CH - 1))
                last_bu = TE.inc(inst)

                SX(TE, last_bu)
                SX(VE, rest_last_prev[m])
                bu_evac[m] = S(lambda mm=m, p=pt, b=base:
                               nc.scalar.activation(
                                   vv[mm][:, :], p[:, b:b + 1024],
                                   AF.Identity))

                cseg = cS[:, m * NSUP:(m + 1) * NSUP]
                sseg = sS[:, m * NSUP:(m + 1) * NSUP]
                ur = vv[m][:, 0:NSUP]
                ui = vv[m][:, NSUP:2 * NSUP]
                xr = xsre[:, m * NSUP:(m + 1) * NSUP]
                xi2 = xsim[:, m * NSUP:(m + 1) * NSUP]
                VX(SE, bu_evac[m])
                V(lambda a=xr, b=ur, c2=cseg:
                  nc.vector.tensor_tensor(a, b, c2, ALU.mult))
                V(lambda a=xi2, b=ur, c2=sseg:
                  nc.vector.tensor_tensor(a, b, c2, ALU.mult))
                V(lambda a=ur, b=ui, c2=sseg:
                  nc.vector.tensor_tensor(a, b, c2, ALU.mult))
                V(lambda a=ui, c2=cseg:
                  nc.vector.tensor_tensor(a, a, c2, ALU.mult))
                V(lambda a=ur, b=xr: nc.vector.tensor_tensor(
                    a, b, a, ALU.add))
                V(lambda a=ui, b=xi2: nc.vector.tensor_tensor(
                    a, a, b, ALU.subtract))
                if m >= 2:
                    emit_rest(m - 2)
            if not has_prev:
                emit_rest(CH - 2)
                emit_rest(CH - 1)
                rest_last_prev = list(rest_last)
                if k + 1 < NSTEP:
                    if DEBUG and k == 0:
                        VX(DBG)
                    emit_rot(k + 1)
            else:
                emit_rest_deferred = emit_rest
            bu_evac_prev = bu_evac
            pp_release(bu_pp[0], "SE", bu_evac[4])
            pp_release(bu_pp[1], "SE", bu_evac[4])
            pp_release(bu_pp[2], "SE", bu_evac[5])
            pp_release(bu_pp[3], "SE", bu_evac[5])
            chain_done = VE.n
            if DEBUG and k == 0:
                dump("dbg_xsre", xsre[:, :], VE, chain_done)
                dump("dbg_xsim", xsim[:, :], VE, chain_done)
            piece_done(jb0, last_bu)
            piece_done(jb1, last_bu)
            prev_cp = [enq_piece("cre"), enq_piece("cim")]
            if k + 1 < NSTEP:
                next_b = [enq_piece("bre"), enq_piece("bim")]
            nw1 = [[0] * 4, [0] * 4]
            nw2 = [[0] * 4, [0] * 4]
            for sub in range(2):
                for q in range(4):
                    nw1[sub][q] = enq_piece("w1", q)
                for q in range(4):
                    nw2[sub][q] = enq_piece("w2", q)
            prev_w = (nw1, nw2)
            pump()

        if has_prev:
            new_gev = [[0] * 8, [0] * 8]
            resid_done = [0, 0]
            outa_resid = [0, 0]
            w1p, w2p = cur_w
            for sub in range(2):
                if sub == 0 and st_first_ffn:
                    SX(WST[0])
                    SX(WST[1])
                    st_first_ffn = False
                for q in range(4):
                    jw = w1p[sub][q]
                    pump()
                    piece_wait(jw)
                    inst = None
                    gm = 0
                    for ml in range(8):
                        bk = pp_acquire()
                        bank = PP[bk]
                        for kk in range(CH):
                            inst = nc.tensor.matmul(
                                bank[:, 0:256],
                                pslot(jw)[:, kk * 1024 + ml * 128:
                                          kk * 1024 + (ml + 1) * 128],
                                xnf_p[:, kk * NSUP + sub * NSUB:
                                      kk * NSUP + sub * NSUB + NSUB],
                                start=(kk == 0), stop=(kk == CH - 1))
                        gm = TE.inc(inst)
                        SX(TE, gm)
                        mi = q * 8 + ml
                        ev = S(lambda m2=mi, b=bank: nc.scalar.activation(
                            h1[:, m2 * NSUB:(m2 + 1) * NSUB],
                            b[:, 0:256], AF.Relu,
                            bias=b1_tab[:, m2:m2 + 1]))
                        pp_release(bk, "SE", ev)
                        h1_ev_last = ev
                    piece_done(jw, gm)
                w1_last[(k - 1) % 2] = TE.n
                if DEBUG and k == 1 and sub == 0:
                    dump("dbg_h1", h1[:, :], SE, h1_ev_last)
                if DEBUG and k == 1 and sub == 1:
                    SX(DBG)

                TX(SE, bu_evac_prev[6])
                TX(SE, bu_evac_prev[7])
                if sub == 1:
                    TX(VE, outa_resid[0])
                    TX(VE, outa_resid[1])
                TX(SE, h1_ev_last)
                SX(VE, chain_done)
                ppb = [pp_acquire() for _ in range(4)]
                w2bank = ([PA[:, j * 512:j * 512 + 256] for j in range(4)]
                          + [PB[:, b * 512:b * 512 + 256] for b in ppb])
                for q in range(4):
                    jw = w2p[sub][q]
                    pump()
                    piece_wait(jw)
                    inst = None
                    gm = 0
                    for m in range(CH):
                        if q == 0 and sub == 1:
                            TX(SE, new_gev[0][m])
                        for kl in range(CH):
                            inst = nc.tensor.matmul(
                                w2bank[m],
                                pslot(jw)[:, kl * 1024 + m * 128:
                                          kl * 1024 + (m + 1) * 128],
                                h1[:, (q * 8 + kl) * NSUB:
                                   (q * 8 + kl + 1) * NSUB],
                                start=(q == 0 and kl == 0),
                                stop=(q == 3 and kl == CH - 1),
                                skip_group_check=True)
                        if q == 3:
                            gm = TE.inc(inst)
                            SX(TE, gm)
                            new_gev[sub][m] = S(
                                lambda bnk=w2bank[m], mm=m:
                                nc.scalar.activation(
                                    g_b[:, mm * NSUB:(mm + 1) * NSUB], bnk,
                                    AF.Silu, bias=b2_tab[:, mm:mm + 1]))
                    if q < 3:
                        gm = TE.inc(inst)
                    piece_done(jw, gm)
                for jj, b in enumerate(ppb):
                    pp_release(b, "SE", new_gev[sub][4 + jj])

                if DEBUG and k == 1 and sub == 0:
                    dump("dbg_g", g_b[:, :], SE, new_gev[0][CH - 1])
                    VX(DBG)
                tmh = [0, 0]
                for hf in range(2):
                    VX(SE, new_gev[sub][hf * 4 + 3])
                    for mm in range(hf * 4, hf * 4 + 4):
                        V(lambda m2=mm:
                          nc.vector.scalar_tensor_tensor(
                              s_b[:, m2 * NSUP + sub * NSUB:
                                  m2 * NSUP + sub * NSUB + NSUB],
                              xnf_p[:, m2 * NSUP + sub * NSUB:
                                    m2 * NSUP + sub * NSUB + NSUB],
                              d_tab[:, m2:m2 + 1],
                              s_b[:, m2 * NSUP + sub * NSUB:
                                  m2 * NSUP + sub * NSUB + NSUB],
                              ALU.mult, ALU.add))
                    s_ap = sbap(s_b, sub * NSUB + hf * 4 * NSUP,
                                [[CH * NSUP, 128], [NSUP, 4], [1, NSUB]])
                    g_ap = sbap(g_b, hf * 4 * NSUB,
                                [[CH * NSUB, 128], [NSUB, 4], [1, NSUB]])
                    tmh[hf] = V(lambda sa=s_ap, ga=g_ap:
                                nc.vector.tensor_tensor(ga, sa, ga, ALU.mult))

                rm_last = 0
                for halfc in range(2):
                    for tbl in range(2):
                        tb = sub * 2 + tbl
                        if sub == 0:
                            bk2 = 2 * halfc + tbl
                            pdst = PA[:, bk2 * 512:bk2 * 512 + 512]
                            TX(SE, new_gev[0][bk2])
                        else:
                            bk = pp_acquire()
                            pdst = PB[:, bk * 512:bk * 512 + 512]
                        TX(VE, tmh[halfc])
                        inst = None
                        for c4 in range(4):
                            c = halfc * 4 + c4
                            inst = nc.tensor.transpose(
                                pdst[:, c4 * 128:c4 * 128 + 128],
                                g_b[:, c * NSUB + tbl * 128:
                                    c * NSUB + tbl * 128 + 128],
                                ident[:, :])
                        gm = TE.inc(inst)
                        VX(TE, gm)
                        xo = xsl[(k - 1) % 2][:, tb * H + halfc * 512:
                                              tb * H + halfc * 512 + 512]
                        rm = V(lambda a=xo, b2=pdst:
                               nc.vector.tensor_tensor(a, a, b2, ALU.add))
                        if sub == 0:
                            outa_resid[bk2 // 2] = rm
                        else:
                            pp_release(bk, "VE", rm)
                        rm_last = rm

                resid_done[sub] = rm_last
                VE.wait(nc.scalar, rm_last)
                srcx = sbap(xsl[(k - 1) % 2], sub * 2 * H,
                            [[XEXT, 128], [H, 2], [1, H]])
                out_dma[(k - 1, sub)] = SYO[sub].inc(
                    nc.scalar.dma_start(out_dst_ap(k - 1, sub), srcx))
                if k + 1 < NSTEP:
                    sl = (k + 1) % 2
                    SYO[sub].wait(nc.sync)
                    XS[sl][sub].inc(nc.sync.dma_start(
                        sbap(xsl[sl], sub * 2 * H,
                             [[XEXT, 128], [H, 2], [1, H]]),
                        x_src_half(k + 1, sub)))
                if sub == 0 and has_cur:
                    emit_rest_deferred(CH - 2)
                    emit_rest_deferred(CH - 1)
                    rest_last_prev = list(rest_last)
                    chain_done = VE.n
                if k + 1 < NSTEP and sub == 0:
                    emit_rot(k + 1)
                    emit_ln_v(k + 1, 0)
            g_ev = new_gev
            outa_resid_prev = outa_resid
            du_last[(k - 1) % 2] = VE.n
            if k + 1 < NSTEP:
                emit_ln_sqrt(k + 1, 0)
                emit_norm01(k + 1)

        if k == 0:
            SX(WST[0])
            SX(WST[1])
            bc_prep([(cre_d, sc_c[0], 1.0), (cim_d, sc_c[1], -1.0)],
                    h1, CI * NSUB)
            for chx in BST:
                chx.wait(nc.sync)
            WST[0].wait(nc.sync)
            WST[1].wait(nc.sync)
            pump_hold.discard("cre")
            pump_hold.discard("cim")
            pump_hold.discard("w1")
            pump()
            w2_cast()
            for chx in WST:
                chx.wait(nc.sync)
            pump_hold.clear()
            pump()
        if k + 1 < NSTEP:
            if not has_prev:
                emit_ln_v(k + 1, 0)
                emit_ln_sqrt(k + 1, 0)
                emit_norm01(k + 1)
            emit_ln_v(k + 1, 1)

    for chx in ([SYP] + SYO + WS + XS[0] + XS[1] + BT + BST + GDP + WST
                + [DBG]):
        chx.wait(nc.sync)
    es.close()
    return nc


_CACHED = {}


def kernel(L0_attn_inputs, L0_attn_nu_log, L0_attn_theta_log, L0_attn_B_re,
           L0_attn_B_im, L0_attn_C_re, L0_attn_C_im, L0_attn_D,
           L0_attn_gamma_log, L0_ffn_W1, L0_ffn_b1, L0_ffn_W2, L0_ffn_b2):
    x = np.ascontiguousarray(np.asarray(L0_attn_inputs, dtype=np.float32))
    common = dict(
        nu_log=np.ascontiguousarray(L0_attn_nu_log, dtype=np.float32),
        theta_log=np.ascontiguousarray(L0_attn_theta_log, dtype=np.float32),
        B_re=np.ascontiguousarray(L0_attn_B_re, dtype=np.float32),
        B_im=np.ascontiguousarray(L0_attn_B_im, dtype=np.float32),
        C_re=np.ascontiguousarray(L0_attn_C_re, dtype=np.float32),
        C_im=np.ascontiguousarray(L0_attn_C_im, dtype=np.float32),
        Dv=np.ascontiguousarray(L0_attn_D, dtype=np.float32),
        gamma_log=np.ascontiguousarray(L0_attn_gamma_log, dtype=np.float32),
        W1=np.ascontiguousarray(L0_ffn_W1, dtype=np.float32),
        b1=np.ascontiguousarray(L0_ffn_b1, dtype=np.float32),
        W2=np.ascontiguousarray(L0_ffn_W2, dtype=np.float32),
        b2=np.ascontiguousarray(L0_ffn_b2, dtype=np.float32),
    )
    if "nc" not in _CACHED:
        _CACHED["nc"] = build_program()
    nc = _CACHED["nc"]
    in_maps = [dict(x=np.ascontiguousarray(x[c]), **common) for c in range(8)]
    res = run_bass_kernel_spmd(nc, in_maps, core_ids=list(range(8)))
    return np.stack([np.asarray(res.results[c]["out"]) for c in range(8)],
                    axis=0)
